# version 23
# baseline (speedup 1.0000x reference)
import numpy as np
import ml_dtypes
import concourse.bass as bass
import concourse.mybir as mybir
from concourse.bass_utils import run_bass_kernel_spmd

F32 = mybir.dt.float32
BF16 = mybir.dt.bfloat16
I32 = mybir.dt.int32
AF = mybir.ActivationFunctionType
ALU = mybir.AluOpType
AX = mybir.AxisListType
BF = ml_dtypes.bfloat16

T = 2048
D = 1024
NH = 16
FT = 16
HD = 64
HID = 3072
NCK = 16
NHB = 24
EPS = 1e-5
C_OFF = float(np.sqrt(2.0) / 4.0)
OFFD = [0] * 16
for _d in range(1, 16):
    OFFD[_d] = 16 + sum(16 - dd for dd in range(1, _d))

REAL_ENGS = ("sync", "tensor", "vector", "scalar", "pool")
CHAIN_ENGS = {"vector": "vch", "scalar": "sch", "pool": "pch"}
EMIT_METHODS = frozenset("""tensor_tensor tensor_add tensor_sub tensor_mul tensor_max
 tensor_scalar tensor_single_scalar tensor_scalar_add tensor_scalar_sub
 tensor_scalar_mul tensor_scalar_max tensor_scalar_min tensor_copy tensor_reduce
 reduce_sum reduce_max scalar_tensor_tensor tensor_tensor_scan memset memzero
 reciprocal bn_stats bn_aggr select transpose iota activation copy mul add square
 sqrt sign engine_nop stream_shuffle pool pool_avg pool_max tensor_relu""".split())


class Fenced:

    def __init__(self, obj, sem, holder, mode="all"):
        self._o = obj
        self._sem = sem
        self._h = holder
        self._mode = mode
        self._first = True
        self.last_inst = None

    def __getattr__(self, name):
        f = getattr(self._o, name)
        if name not in EMIT_METHODS:
            return f
        def wrap(*a, **kw):
            if self._h[0] > 0 and (self._mode == "all" or self._first):
                self._o.wait_ge(self._sem, self._h[0])
            self._first = False
            inst = f(*a, **kw)
            if self._mode == "all":
                inst.then_inc(self._sem, 1)
                self._h[0] += 1
            self.last_inst = inst
            return inst
        return wrap
SEM_NAMES = list(REAL_ENGS) + ["c", "x0", "x1", "x2", "x3", "wa", "gu0", "gu1",
                               "wdA", "wdB", "ox0", "ox1", "ou0", "ou1", "tap",
                               "vch", "sch", "pch"]


class Em:

    def __init__(self):
        self.mode = "count"
        self.cnt = {e: 0 for e in SEM_NAMES}
        self.val = {}
        self.key_eng = {}
        self.emit_eng = None
        self.obj = None
        self.sems = None
        self.waited = {e: {} for e in REAL_ENGS}
        self.chain = {e: [0] for e in CHAIN_ENGS}

    def op(self, eng, key, waits=(), fn=None, incs=1, self_inc=False, sem=None,
           fence="all"):
        sem = sem or eng
        if self.mode == "count":
            for w in waits:
                assert w in self.val, f"wait on unknown key {w} (from {key})"
            self.cnt[sem] += incs
            assert key not in self.val, f"dup key {key}"
            self.val[key] = (sem, self.cnt[sem])
            self.key_eng[key] = eng
            return
        if eng != self.emit_eng:
            return
        wd = self.waited[self.emit_eng]
        for w in waits:
            src, v = self.val[w]
            if self.key_eng[w] == eng and src in REAL_ENGS:
                continue
            if wd.get(src, -1) >= v:
                continue
            self.obj.wait_ge(self.sems[src], v)
            wd[src] = v
        if fn is not None:
            if eng in CHAIN_ENGS:
                ch_sem = self.sems[CHAIN_ENGS[eng]]
                holder = self.chain[eng]
                fe = Fenced(self.obj, ch_sem, holder, mode=fence)
                last = fn(fe)
                if not self_inc:
                    assert last is not None, f"fn for {key} returned no instruction"
                    if fence != "all" and fe.last_inst is not None:
                        fe.last_inst.then_inc(ch_sem, 1)
                        holder[0] += 1
                    self.obj.wait_ge(ch_sem, holder[0])
                    self.obj.nop().then_inc(self.sems[sem], incs)
            else:
                last = fn(self.obj)
                if not self_inc:
                    assert last is not None, f"fn for {key} returned no instruction"
                    last.then_inc(self.sems[sem], incs)

    def wait_only(self, eng, key, waits):
        if self.mode == "count":
            for w in waits:
                assert w in self.val
            return
        if eng != self.emit_eng:
            return
        wd = self.waited[self.emit_eng]
        for w in waits:
            src, v = self.val[w]
            if wd.get(src, -1) >= v:
                continue
            self.obj.wait_ge(self.sems[src], v)
            wd[src] = v


def _program(em, tb, dbg, ncks=NCK, astages=9999):

    def ap(name, off, dims):
        return bass.AP(tb[name], off, dims)

    def p0(em):
        def f_const(e):
            e.dma_start(tb["identb"][:], tb["d_identb"]).then_inc(em.sems["c"], 16)
            e.dma_start(tb["maskb"][:], tb["d_maskb"]).then_inc(em.sems["c"], 16)
            e.dma_start(tb["gpad"][:], tb["d_gpad"]).then_inc(em.sems["c"], 16)
            e.dma_start(tb["bpad"][:], tb["d_bpad"]).then_inc(em.sems["c"], 16)
        em.op("sync", "cdma", fn=f_const, incs=64, self_inc=True, sem="c")

        def f_init(e):
            return e.memset(tb["eps_sb"][:], EPS)
        em.op("pool", "init", fn=f_init)

        def f_wdma(e):
            e.dma_start(tb["wq_bf"][:], tb["d_wqt"]).then_inc(em.sems["wa"], 16)
            e.dma_start(tb["wk_bf"][:], tb["d_wkt"]).then_inc(em.sems["wa"], 16)
            e.dma_start(tb["wv_bf"][:], tb["d_wvt"]).then_inc(em.sems["wa"], 16)
            e.dma_start(tb["wo_bf"][:], tb["d_wot"]).then_inc(em.sems["wa"], 16)
        em.op("pool", "wdma", fn=f_wdma, incs=64, self_inc=True, sem="wa")
        for hb in (0, 1):
            def f_gu(e, hb=hb):
                par = hb % 2
                e.dma_start(tb["wg_bf"][:, 1024 * par:1024 * par + 1024],
                            tb["d_wgt"][hb]).then_inc(em.sems[f"gu{hb % 2}"], 16)
                e.dma_start(tb["wu_bf"][:, 1024 * par:1024 * par + 1024],
                            tb["d_wut"][hb]).then_inc(em.sems[f"gu{hb % 2}"], 16)
            em.op("pool", f"gu_dma{hb}", fn=f_gu, incs=32, self_inc=True, sem=f"gu{hb % 2}")

        def mk_xdma(c):
            def f(e):
                e.dma_start(tb["x_t"][:, 1024 * (c % 4):1024 * (c % 4) + 1024],
                            tb["d_x"][128 * c:128 * c + 128, :]).then_inc(em.sems[f"x{c % 4}"], 16)
            return f

        def mk_p0sq(c):
            def f(e):
                return e.activation(tb["xsq"][:, 1024 * (c % 2):1024 * (c % 2) + 1024],
                                    tb["x_t"][:, 1024 * (c % 4):1024 * (c % 4) + 1024],
                                    AF.Square)
            return f

        def mk_p0stat(c):
            par = 8 * (c % 2)
            def f(e):
                xt = tb["x_t"][:, 1024 * (c % 4):1024 * (c % 4) + 1024]
                st = tb["p0st"]
                e.reduce_sum(st[:, par + 0:par + 1], xt, axis=AX.X)
                e.reduce_sum(st[:, par + 1:par + 2],
                             tb["xsq"][:, 1024 * (c % 2):1024 * (c % 2) + 1024], axis=AX.X)
                e.tensor_scalar_mul(st[:, par + 2:par + 3], st[:, par + 0:par + 1], 1.0 / D)
                e.tensor_mul(st[:, par + 3:par + 4], st[:, par + 2:par + 3],
                             st[:, par + 2:par + 3])
                return e.scalar_tensor_tensor(st[:, par + 4:par + 5], st[:, par + 1:par + 2],
                                              1.0 / D, st[:, par + 3:par + 4],
                                              ALU.mult, ALU.subtract)
            return f

        def mk_p0rs(c):
            par = 8 * (c % 2)
            def f(e):
                return e.activation(tb["p0st"][:, par + 5:par + 6], tb["p0st"][:, par + 4:par + 5],
                                    AF.Sqrt, bias=tb["eps_sb"][:, 0:1])
            return f

        def mk_p0xc(c):
            par = 8 * (c % 2)
            def f(e):
                st = tb["p0st"]
                e.reciprocal(st[:, par + 6:par + 7], st[:, par + 5:par + 6])
                return e.tensor_scalar(tb["xn_nat"][:, 1024 * (c % 2):1024 * (c % 2) + 1024],
                                       tb["x_t"][:, 1024 * (c % 4):1024 * (c % 4) + 1024],
                                       st[:, par + 2:par + 3], st[:, par + 6:par + 7],
                                       ALU.subtract, ALU.mult)
            return f

        def mk_p0t(c):
            def f(e):
                pb = (tb["PB0"], tb["PB1"]) if c % 2 == 0 else (tb["PB2"], tb["PB3"])
                last = None
                for dk in range(8):
                    last = e.matmul(pb[dk // 4][:, 128 * (dk % 4):128 * (dk % 4) + 128],
                                    tb["xn_nat"][:, 1024 * (c % 2) + 128 * dk:
                                                 1024 * (c % 2) + 128 * dk + 128],
                                    tb["identb"][:], start=True, stop=True)
                return last
            return f

        def mk_p0evac(c):
            def f(e):
                names = ("PB0", "PB1") if c % 2 == 0 else ("PB2", "PB3")
                last = None
                for i, pn in enumerate(names):
                    last = e.activation(
                        ap("xnT", 2048 * (4 * i) + 128 * c, [[16384, 128], [2048, 4], [1, 128]]),
                        tb[pn][:], AF.Copy)
                return last
            return f

        for i in range(NCK + 3):
            if i < NCK:
                w = [f"p0xc{i-4}"] if i >= 4 else []
                em.op("sync", f"xdma{i}", waits=w, fn=mk_xdma(i), incs=16,
                      self_inc=True, sem=f"x{i % 4}")
                w = [f"xdma{i}"] + ([f"p0stat{i-2}"] if i >= 2 else [])
                em.op("scalar", f"p0sq{i}", waits=w, fn=mk_p0sq(i), fence="once")
                em.op("vector", f"p0stat{i}", waits=[f"p0sq{i}"], fn=mk_p0stat(i))
            c1 = i - 1
            if 0 <= c1 < NCK:
                em.op("scalar", f"p0rs{c1}", waits=[f"p0stat{c1}", "init"], fn=mk_p0rs(c1),
                      fence="once")
                w = [f"p0rs{c1}"] + ([f"p0t{c1-2}"] if c1 >= 2 else [])
                em.op("vector", f"p0xc{c1}", waits=w, fn=mk_p0xc(c1))
                w = [f"p0xc{c1}", "cdma"] + ([f"p0evac{c1-2}"] if c1 >= 2 else [])
                em.op("tensor", f"p0t{c1}", waits=w, fn=mk_p0t(c1))
            c2 = i - 2
            if 0 <= c2 < NCK:
                em.op("scalar", f"p0evac{c2}", waits=[f"p0t{c2}"], fn=mk_p0evac(c2),
                      fence="once")

    def attn(em):
        _n = [0]
        _orig_op = em.op

        def _op(*a, **kw):
            _n[0] += 1
            if _n[0] > astages:
                return
            return _orig_op(*a, **kw)
        em = type("EmShim", (), {"op": staticmethod(_op), "sems": None,
                                 "wait_only": em.wait_only})()
        em.sems = _orig_op.__self__.sems

        def f_init2(e):
            e.memset(tb["S_f32"][:], 0.0)
            e.memset(tb["S_bf"][:], 0.0)
            e.memset(tb["qhat"][:], 0.0)
            e.memset(tb["khat2"][:], 0.0)
            e.memset(tb["phiq"][:], 0.0)
            e.memset(tb["phik"][:], 0.0)
            e.memset(tb["phiqT"][:], 0.0)
            e.memset(ap("v_bf", 64, [[2080, 128], [65, 16]]), 1.0)
            e.memset(ap("v_bf", 1040 + 64, [[2080, 128], [65, 16]]), 1.0)
            e.tensor_scalar_mul(tb["gpad"][:], tb["gpad"][:], 0.5)
            return e.tensor_scalar_mul(tb["bpad"][:], tb["bpad"][:], 0.5)
        em.op("pool", "init2", waits=["cdma"], fn=f_init2)

        def mk_qkvA(c):
            def f(e):
                last = None
                for dk in range(8):
                    lhs = tb["xnT"][:, 2048 * dk + 128 * c:2048 * dk + 128 * c + 128]
                    e.matmul(tb["PB4"][:, 0:256], lhs,
                             tb["wq_bf"][:, 256 * dk:256 * dk + 256],
                             start=dk == 0, stop=dk == 7)
                    e.matmul(tb["PB5"][:, 0:256], lhs,
                             tb["wk_bf"][:, 256 * dk:256 * dk + 256],
                             start=dk == 0, stop=dk == 7)
                    last = e.matmul(tb["PB3"][:, 0:512], lhs,
                                    tb["wv_bf"][:, 1024 * dk:1024 * dk + 512],
                                    start=dk == 0, stop=dk == 7)
                return last
            return f

        def mk_vh2(c):
            def f(e):
                last = None
                for dk in range(8):
                    lhs = tb["xnT"][:, 2048 * dk + 128 * c:2048 * dk + 128 * c + 128]
                    last = e.matmul(tb["PB3"][:, 0:512], lhs,
                                    tb["wv_bf"][:, 1024 * dk + 512:1024 * dk + 1024],
                                    start=dk == 0, stop=dk == 7)
                return last
            return f

        def mk_vev(c, half):
            def f(e):
                return e.activation(
                    ap("v_bf", 1040 * (c % 2) + 65 * 8 * half,
                       [[2080, 128], [65, 8], [1, 64]]),
                    ap("PB3", 0, [[512, 128], [64, 8], [1, 64]]), AF.Copy)
            return f

        def mk_qkev(c):
            def f(e):
                e.activation(tb["qsq"][:, 0:256], tb["PB4"][:, 0:256], AF.Square)
                e.activation(tb["qsq"][:, 256:512], tb["PB5"][:, 0:256], AF.Square)
                e.activation(tb["qk_sb"][:, 0:256], tb["PB4"][:, 0:256], AF.Copy)
                return e.activation(tb["qk_sb"][:, 256:512], tb["PB5"][:, 0:256], AF.Copy)
            return f

        def mk_featln(c):
            def f(e):
                st = tb["fst"]
                e.reduce_sum(st[:, 0:32],
                             tb["qk_sb"][:].rearrange("p (a b) -> p a b", b=16), axis=AX.X)
                e.reduce_sum(st[:, 32:64],
                             tb["qsq"][:].rearrange("p (a b) -> p a b", b=16), axis=AX.X)
                e.tensor_scalar_mul(st[:, 64:96], st[:, 0:32], 1.0 / FT)
                e.tensor_mul(st[:, 96:128], st[:, 64:96], st[:, 64:96])
                return e.scalar_tensor_tensor(st[:, 128:160], st[:, 32:64], 1.0 / FT,
                                              st[:, 96:128], ALU.mult, ALU.subtract)
            return f

        def mk_rsq(c):
            def f(e):
                return e.activation(tb["fst"][:, 160:192], tb["fst"][:, 128:160], AF.Sqrt,
                                    bias=tb["eps_sb"][:, 0:1])
            return f

        def mk_qc(c):
            def f(e):
                st = tb["fst"]
                e.reciprocal(st[:, 192:224], st[:, 160:192])
                e.tensor_tensor(ap("qc", 0, [[512, 128], [16, 32], [1, 16]]),
                                ap("qk_sb", 0, [[512, 128], [16, 32], [1, 16]]),
                                ap("fst", 64, [[256, 128], [1, 32], [0, 16]]),
                                ALU.subtract)
                return e.tensor_tensor(ap("qc", 0, [[512, 128], [16, 32], [1, 16]]),
                                       ap("qc", 0, [[512, 128], [16, 32], [1, 16]]),
                                       ap("fst", 192, [[256, 128], [1, 32], [0, 16]]),
                                       ALU.mult)
            return f

        def mk_gb(c):
            def f(e):
                e.tensor_tensor(tb["qc"][:], tb["qc"][:], tb["gpad"][:], ALU.mult)
                e.tensor_tensor(ap("qhat", 0, [[2048, 128], [128, 16], [1, 16]]),
                                ap("qc", 0, [[512, 128], [16, 16], [1, 16]]),
                                ap("bpad", 0, [[512, 128], [16, 16], [1, 16]]),
                                ALU.add)
                e.tensor_tensor(ap("khat2", 0, [[512, 128], [128, 4], [32, 4], [1, 16]]),
                                ap("qc", 256, [[512, 128], [64, 4], [16, 4], [1, 16]]),
                                ap("bpad", 256, [[512, 128], [64, 4], [16, 4], [1, 16]]),
                                ALU.add)
                return e.tensor_tensor(tb["khd"][:], tb["qc"][:, 256:512],
                                       tb["bpad"][:, 256:512], ALU.add)
            return f

        def mk_phiq(c):
            def f(e):
                return e.tensor_tensor(
                    bass.AP(tb["phiq"], 0, [[4096, 128], [256, 16], [16, 16], [1, 16]]),
                    bass.AP(tb["qhat"], 0, [[2048, 128], [128, 16], [1, 16], [0, 16]]),
                    bass.AP(tb["qhat"], 0, [[2048, 128], [128, 16], [0, 16], [1, 16]]),
                    ALU.mult)
            return f

        def mk_phik(c):
            par = 4096 * (c % 2)
            def f(e):
                return e.tensor_tensor(
                    bass.AP(tb["phik"], par, [[8192, 128], [256, 16], [16, 16], [1, 16]]),
                    bass.AP(tb["khd"], 0, [[256, 128], [16, 16], [1, 16], [0, 16]]),
                    bass.AP(tb["khd"], 0, [[256, 128], [16, 16], [0, 16], [1, 16]]),
                    ALU.mult)
            return f

        def mk_trsq_a():
            def f(e):
                last = None
                for g in range(12):
                    if g < 8:
                        pb, col = tb["PB7"], 128 * g
                    else:
                        pb, col = tb["PB6"], 512 + 128 * (g - 8)
                    src = 128 * g - 32 * (g % 4)
                    last = e.transpose(pb[:, col:col + 128],
                                       tb["qhat"][:, src:src + 128],
                                       tb["identb"][:])
                return last
            return f

        def mk_trsq_b():
            def f(e):
                last = None
                for g in range(12, 16):
                    src = 128 * g - 32 * (g % 4)
                    last = e.transpose(tb["PB6"][:, 128 * (g - 12):128 * (g - 12) + 128],
                                       tb["qhat"][:, src:src + 128],
                                       tb["identb"][:])
                return last
            return f

        def mk_trsevq():
            def f(e):
                e.activation(tb["qhatT"][:, 0:1024], tb["PB7"][:], AF.Copy)
                e.activation(tb["qhatT"][:, 1024:1536], tb["PB6"][:, 512:1024], AF.Copy)
                return e.activation(tb["qhatT"][:, 1536:2048], tb["PB6"][:, 0:512],
                                    AF.Copy)
            return f

        def mk_trsk2():
            def f(e):
                last = None
                for a in range(4):
                    last = e.transpose(tb["PB6"][:, 128 * a:128 * a + 128],
                                       tb["khat2"][:, 128 * a:128 * a + 128],
                                       tb["identb"][:])
                return last
            return f

        def mk_trskev():
            def f(e):
                return e.tensor_copy(tb["khatT2"][:], tb["PB6"][:, 0:512])
            return f

        AB = ("PB0", "PB1", "PB2")

        def mk_amm1(c):
            def f(e):
                last = None
                for a in range(3):
                    last = e.matmul(tb[AB[a]][:, 0:512],
                                    tb["khatT2"][:, 128 * a:128 * a + 128],
                                    tb["qhatT"][:, 512 * a:512 * a + 512],
                                    start=True, stop=True)
                return last
            return f

        def mk_amm2(c):
            def f(e):
                return e.matmul(tb["PB0"][:, 0:512],
                                tb["khatT2"][:, 384:512],
                                tb["qhatT"][:, 1536:2048],
                                start=True, stop=True)
            return f

        def mk_asq1(c):
            def f(e):
                last = None
                for i, pbn in enumerate(AB):
                    last = e.activation(tb["Asq"][:, 512 * i:512 * i + 512],
                                        tb[pbn][:], AF.Square)
                return last
            return f

        def mk_asq2(c):
            def f(e):
                return e.activation(tb["Asq"][:, 1536:2048], tb["PB0"][:],
                                    AF.Square)
            return f

        def mk_amask(c):
            def f(e):
                return e.tensor_tensor(
                    tb["Asq"][:].rearrange("p (h t) -> p h t", t=128),
                    tb["Asq"][:].rearrange("p (h t) -> p h t", t=128),
                    bass.AP(tb["maskb"], 0, [[128, 128], [0, 16], [1, 128]]),
                    ALU.mult)
            return f

        def mk_phit1(c):
            def f(e):
                last = None
                for b in range(16):
                    pb = tb["PB6"] if b < 8 else tb["PB7"]
                    last = e.transpose(pb[:, 128 * (b % 8):128 * (b % 8) + 128],
                                       tb["phiq"][:, 128 * b:128 * b + 128],
                                       tb["identb"][:])
                return last
            return f

        def mk_phitev1(c):
            def f(e):
                e.activation(tb["phiqT"][:, 0:1024], tb["PB6"][:], AF.Copy)
                return e.activation(tb["phiqT"][:, 1024:2048], tb["PB7"][:], AF.Copy)
            return f

        def mk_phit2(c):
            def f(e):
                last = None
                for b in range(16, 32):
                    pb = tb["PB6"] if b < 24 else tb["PB7"]
                    last = e.transpose(pb[:, 128 * (b % 8):128 * (b % 8) + 128],
                                       tb["phiq"][:, 128 * b:128 * b + 128],
                                       tb["identb"][:])
                return last
            return f

        def mk_phitev2(c):
            def f(e):
                e.activation(tb["phiqT"][:, 2048:3072], tb["PB6"][:], AF.Copy)
                return e.activation(tb["phiqT"][:, 3072:4096], tb["PB7"][:], AF.Copy)
            return f

        def numreg(h):
            if h < 7:
                return "PB0", 65 * h
            if h < 14:
                return "PB1", 65 * (h - 7)
            return "PB2", 65 * (h - 14)

        def mk_num(c):
            def f(e):
                last = None
                vb = 1040 * (c % 2)
                for h in range(16):
                    pbn, col = numreg(h)
                    reg = tb[pbn][:, col:col + 65]
                    e.matmul(reg, tb["Asq"][:, 128 * h:128 * h + 128],
                             ap("v_bf", vb + 65 * h, [[2080, 128], [1, 65]]),
                             start=True, stop=False)
                    e.matmul(reg, tb["phiqT"][:, 256 * h:256 * h + 128],
                             tb["S_bf"][:, 130 * h:130 * h + 65], start=False, stop=False)
                    last = e.matmul(reg, tb["phiqT"][:, 256 * h + 128:256 * h + 256],
                                    tb["S_bf"][:, 130 * h + 65:130 * h + 130],
                                    start=False, stop=True)
                return last
            return f

        def mk_y(c):
            def f(e):
                last = None
                for bi, (pbn, h0, nh) in enumerate((("PB0", 0, 7), ("PB1", 7, 7),
                                                    ("PB2", 14, 2))):
                    e.tensor_scalar_add(tb["den"][:, h0:h0 + nh],
                                        bass.AP(tb[pbn], 64, [[512, 128], [65, nh]]), EPS)
                    e.reciprocal(tb["den"][:, 16 + h0:16 + h0 + nh],
                                 tb["den"][:, h0:h0 + nh])
                    last = e.tensor_tensor(
                        ap("y_nat", 64 * h0, [[1024, 128], [64, nh], [1, 64]]),
                        bass.AP(tb[pbn], 0, [[512, 128], [65, nh], [1, 64]]),
                        ap("den", 16 + h0, [[32, 128], [1, nh], [0, 64]]),
                        ALU.mult)
                return last
            return f

        def smm_reg(e, c, r, col, pbn):
            par = 4096 * (c % 2)
            vb = 1040 * (c % 2)
            return e.matmul(tb[pbn][:, col:col + 65],
                            tb["phik"][:, par + 128 * r:par + 128 * r + 128],
                            ap("v_bf", vb + 65 * (r // 2), [[2080, 128], [1, 65]]),
                            start=True, stop=True)

        SMM_WAVES = ((0, 7), (7, 7), (14, 7), (21, 7), (28, 4))
        SMM_BANKS = ("PB4", "PB3", "PB4", "PB2", "PB4")

        def mk_smm(c, w):
            r0, nr = SMM_WAVES[w]
            def f(e):
                last = None
                for i in range(nr):
                    last = smm_reg(e, c, r0 + i, 65 * i, SMM_BANKS[w])
                return last
            return f

        def mk_sadd(c, w):
            r0, nr = SMM_WAVES[w]
            def f(e):
                sl = tb["S_f32"][:, 65 * r0:65 * (r0 + nr)]
                return e.tensor_tensor(sl, tb[SMM_BANKS[w]][:, 0:65 * nr], sl, ALU.add)
            return f

        def mk_sbf(c):
            def f(e):
                e.activation(tb["S_bf"][:, 0:1040], tb["S_f32"][:, 0:1040], AF.Copy)
                return e.activation(tb["S_bf"][:, 1040:2080], tb["S_f32"][:, 1040:2080],
                                    AF.Copy)
            return f

        def mk_yt(c):
            def f(e):
                last = None
                for i in range(8):
                    last = e.transpose(tb["PB7"][:, 128 * i:128 * i + 128],
                                       tb["y_nat"][:, 128 * i:128 * i + 128],
                                       tb["identb"][:])
                return last
            return f

        def mk_ytev(c):
            def f(e):
                return e.activation(tb["yT"][:], tb["PB7"][:], AF.Copy)
            return f

        def mk_wo(c):
            def f(e):
                last = None
                for yk in range(8):
                    lhs = tb["yT"][:, 128 * yk:128 * yk + 128]
                    e.matmul(tb["PB0"][:, 0:512], lhs,
                             tb["wo_bf"][:, 1024 * yk:1024 * yk + 512],
                             start=yk == 0, stop=yk == 7)
                    last = e.matmul(tb["PB1"][:, 0:512], lhs,
                                    tb["wo_bf"][:, 1024 * yk + 512:1024 * yk + 1024],
                                    start=yk == 0, stop=yk == 7)
                return last
            return f

        def mk_sevac(c):
            def f(e):
                e.activation(tb["s_sb"][:, 1024 * c:1024 * c + 512], tb["PB0"][:], AF.Copy)
                return e.activation(tb["s_sb"][:, 1024 * c + 512:1024 * c + 1024],
                                    tb["PB1"][:], AF.Copy)
            return f

        em.op("tensor", "qkvA0", waits=["p0evac15", "wdma"], fn=mk_qkvA(0))
        em.op("scalar", "vev1_0", waits=["qkvA0", "init2"], fn=mk_vev(0, 0), fence="once")
        em.op("scalar", "qkev0", waits=["qkvA0"], fn=mk_qkev(0), fence="once")
        em.op("tensor", "vh2_0", waits=["vev1_0"], fn=mk_vh2(0))
        em.op("scalar", "vev2_0", waits=["vh2_0"], fn=mk_vev(0, 1), fence="once")
        em.op("vector", "featln0", waits=["qkev0"], fn=mk_featln(0))
        em.op("scalar", "rsq0", waits=["featln0", "init"], fn=mk_rsq(0), fence="once")
        em.op("vector", "qc0", waits=["rsq0"], fn=mk_qc(0))
        em.op("pool", "gb0", waits=["qc0", "init2"], fn=mk_gb(0))
        em.op("vector", "phiq0", waits=["gb0", "init2"], fn=mk_phiq(0), fence="once")
        em.op("pool", "phik0", waits=["gb0", "init2"], fn=mk_phik(0), fence="once")

        WN = ("a", "b", "c", "d", "e")
        for c in range(ncks):
            nxt = c + 1 < ncks
            if c > 0:
                em.op("tensor", f"smm_e{c-1}", waits=[f"sadd_c{c-1}"], fn=mk_smm(c - 1, 4))
                em.op("vector", f"sadd_e{c-1}", waits=[f"smm_e{c-1}"], fn=mk_sadd(c - 1, 4),
                      fence="once")
                em.op("scalar", f"sbf{c-1}", waits=[f"sadd_e{c-1}"], fn=mk_sbf(c - 1),
                      fence="once")

            w = [f"gb{c}"]
            if c > 0:
                w += [f"phitev2_{c-1}", f"ytev{c-1}"]
            em.op("tensor", f"trsk{c}", waits=w, fn=mk_trsk2())
            em.op("vector", f"trsev{c}", waits=[f"trsk{c}"], fn=mk_trskev(),
                  fence="once")
            em.op("tensor", f"trsq_a{c}", waits=[f"gb{c}"], fn=mk_trsq_a())
            em.op("tensor", f"trsq_b{c}", waits=[f"trsev{c}"], fn=mk_trsq_b())
            em.op("scalar", f"trsevq{c}", waits=[f"trsq_b{c}"], fn=mk_trsevq(),
                  fence="once")
            if nxt:
                w2 = [f"qkev{c}", f"vev2_{c}"]
                if c > 0:
                    w2 += [f"sadd_e{c-1}", f"sadd_b{c-1}"]
                em.op("tensor", f"qkvA{c+1}", waits=w2, fn=mk_qkvA(c + 1))
                em.op("scalar", f"qkev{c+1}", waits=[f"qkvA{c+1}"], fn=mk_qkev(c + 1),
                      fence="once")

            w = [f"trsevq{c}", f"trsev{c}"]
            if c > 0:
                w += [f"sevac{c-1}", f"sadd_d{c-1}"]
            else:
                w += ["p0evac15", f"qkev{c}"]
            em.op("tensor", f"amm1_{c}", waits=w, fn=mk_amm1(c))
            em.op("scalar", f"asq1_{c}", waits=[f"amm1_{c}"], fn=mk_asq1(c), fence="once")
            if nxt:
                em.op("vector", f"featln{c+1}", waits=[f"qkev{c+1}"], fn=mk_featln(c + 1))

            em.op("tensor", f"phit1_{c}", waits=[f"phiq{c}", f"trsevq{c}"],
                  fn=mk_phit1(c))
            em.op("scalar", f"phitev1_{c}", waits=[f"phit1_{c}"], fn=mk_phitev1(c),
                  fence="once")
            if nxt:
                em.op("scalar", f"rsq{c+1}", waits=[f"featln{c+1}"], fn=mk_rsq(c + 1),
                      fence="once")
            em.op("tensor", f"amm2_{c}", waits=[f"asq1_{c}"], fn=mk_amm2(c))
            em.op("scalar", f"asq2_{c}", waits=[f"amm2_{c}"], fn=mk_asq2(c), fence="once")
            if nxt:
                em.op("vector", f"qc{c+1}", waits=[f"rsq{c+1}", f"gb{c}"], fn=mk_qc(c + 1))
                em.op("pool", f"gb{c+1}", waits=[f"qc{c+1}"], fn=mk_gb(c + 1))
                w = [f"qkvA{c+1}"] + ([f"smm_e{c-1}"] if c > 0 else [])
                em.op("scalar", f"vev1_{c+1}", waits=w, fn=mk_vev(c + 1, 0), fence="once")
            em.op("tensor", f"phit2_{c}", waits=[f"phitev1_{c}"], fn=mk_phit2(c))
            em.op("scalar", f"phitev2_{c}", waits=[f"phit2_{c}"], fn=mk_phitev2(c),
                  fence="once")
            em.op("vector", f"amask{c}", waits=[f"asq2_{c}"], fn=mk_amask(c), fence="once")
            if nxt:
                em.op("tensor", f"vh2_{c+1}", waits=[f"vev1_{c+1}"], fn=mk_vh2(c + 1))
                em.op("scalar", f"vev2_{c+1}", waits=[f"vh2_{c+1}"], fn=mk_vev(c + 1, 1),
                      fence="once")
            w = [f"phitev2_{c}", f"amask{c}", f"vev2_{c}"]
            w += [f"sbf{c-1}"] if c > 0 else ["init2"]
            em.op("tensor", f"num{c}", waits=w, fn=mk_num(c))
            em.op("vector", f"y{c}", waits=[f"num{c}"], fn=mk_y(c))

            w = [f"phik{c}"]
            if c == 0:
                w += ["init2"]
            else:
                w += [f"sadd_e{c-1}"]
            if nxt:
                w += [f"qkev{c+1}"]
            em.op("tensor", f"smm_a{c}", waits=w, fn=mk_smm(c, 0))
            em.op("vector", f"sadd_a{c}", waits=[f"smm_a{c}"] + (["init2"] if c == 0 else []),
                  fn=mk_sadd(c, 0), fence="once")
            if nxt:
                em.op("vector", f"phiq{c+1}", waits=[f"gb{c+1}", f"phitev2_{c}"],
                      fn=mk_phiq(c + 1), fence="once")
                w = [f"gb{c+1}"] + ([f"smm_e{c-1}"] if c > 0 else [])
                em.op("pool", f"phik{c+1}", waits=w, fn=mk_phik(c + 1), fence="once")
            em.op("tensor", f"yt{c}", waits=[f"y{c}", f"phitev1_{c}"], fn=mk_yt(c))
            em.op("scalar", f"ytev{c}", waits=[f"yt{c}"], fn=mk_ytev(c), fence="once")
            em.op("tensor", f"smm_b{c}", waits=[f"vev2_{c+1}" if nxt else "vev2_15"],
                  fn=mk_smm(c, 1))
            em.op("vector", f"sadd_b{c}", waits=[f"smm_b{c}"], fn=mk_sadd(c, 1), fence="once")
            em.op("tensor", f"wo{c}", waits=[f"ytev{c}"], fn=mk_wo(c))
            em.op("scalar", f"sevac{c}", waits=[f"wo{c}"], fn=mk_sevac(c), fence="once")
            em.op("tensor", f"smm_c{c}", waits=[f"sadd_a{c}"], fn=mk_smm(c, 2))
            em.op("vector", f"sadd_c{c}", waits=[f"smm_c{c}"], fn=mk_sadd(c, 2), fence="once")
            em.op("tensor", f"smm_d{c}", waits=[f"y{c}"], fn=mk_smm(c, 3))
            em.op("vector", f"sadd_d{c}", waits=[f"smm_d{c}"], fn=mk_sadd(c, 3), fence="once")

        cl = ncks - 1
        em.op("tensor", f"smm_e{cl}", waits=[f"sadd_c{cl}"], fn=mk_smm(cl, 4))
        em.op("vector", f"sadd_e{cl}", waits=[f"smm_e{cl}"], fn=mk_sadd(cl, 4), fence="once")
        em.op("scalar", f"sbf{cl}", waits=[f"sadd_e{cl}"], fn=mk_sbf(cl), fence="once")

    def p2(em):
        def mk_g(hb, half, pbs):
            def f(e):
                par = hb % 2
                last = None
                for dk in range(8):
                    lhs = tb["wg_bf"][:, 1024 * par + 128 * dk:1024 * par + 128 * dk + 128]
                    for n in range(2):
                        tcol = 1024 * half + 512 * n
                        last = e.matmul(tb[pbs[n]][:, 0:512], lhs,
                                        tb["xnT"][:, 2048 * dk + tcol:2048 * dk + tcol + 512],
                                        start=dk == 0, stop=dk == 7)
                return last
            return f

        def mk_u(hb, half):
            def f(e):
                par = hb % 2
                last = None
                for dk in range(8):
                    lhs = tb["wu_bf"][:, 1024 * par + 128 * dk:1024 * par + 128 * dk + 128]
                    for n in range(2):
                        tcol = 1024 * half + 512 * n
                        last = e.matmul(tb[("PB2", "PB3")[n]][:, 0:512], lhs,
                                        tb["xnT"][:, 2048 * dk + tcol:2048 * dk + tcol + 512],
                                        start=dk == 0, stop=dk == 7)
                return last
            return f

        def mk_silu(hb, half, pbs):
            def f(e):
                ga = tb["g_act0"] if half == 0 else tb["g_act1"]
                e.activation(ga[:, 0:512], tb[pbs[0]][:], AF.Silu, scale=1.0 / 127.0)
                return e.activation(ga[:, 512:1024], tb[pbs[1]][:], AF.Silu,
                                    scale=1.0 / 127.0)
            return f

        def mk_hmul(hb, half):
            def f(e):
                ga = tb["g_act0"] if half == 0 else tb["g_act1"]
                last = None
                for n in range(2):
                    tcol = 2048 * hb + 1024 * half + 512 * n
                    last = e.tensor_tensor(tb["hT"][:, tcol:tcol + 512],
                                           ga[:, 512 * n:512 * n + 512],
                                           tb[("PB2", "PB3")[n]][:], ALU.mult)
                return last
            return f

        for hb in range(NHB):
            if hb >= 2:
                def f_gu(e, hb=hb):
                    par = hb % 2
                    e.dma_start(tb["wg_bf"][:, 1024 * par:1024 * par + 1024],
                                tb["d_wgt"][hb]).then_inc(em.sems[f"gu{hb % 2}"], 16)
                    e.dma_start(tb["wu_bf"][:, 1024 * par:1024 * par + 1024],
                                tb["d_wut"][hb]).then_inc(em.sems[f"gu{hb % 2}"], 16)
                em.op("pool", f"gu_dma{hb}", waits=[f"hmul1_{hb-2}"], fn=f_gu,
                      incs=32, self_inc=True, sem=f"gu{hb % 2}")

            w = [f"gu_dma{hb}"]
            if hb == 0:
                w += ["sevac15", "qkev15"]
            else:
                w += [f"silu0_{hb-1}"]
            em.op("tensor", f"tGU_g0_{hb}", waits=w, fn=mk_g(hb, 0, ("PB0", "PB1")))
            w = ["sadd_e15", "vev2_15"] if hb == 0 else [f"hmul1_{hb-1}"]
            em.op("tensor", f"tGU_u0_{hb}", waits=w, fn=mk_u(hb, 0))
            em.op("scalar", f"silu0_{hb}", waits=[f"tGU_g0_{hb}"],
                  fn=mk_silu(hb, 0, ("PB0", "PB1")), fence="once")
            em.op("vector", f"hmul0_{hb}", waits=[f"silu0_{hb}", f"tGU_u0_{hb}"],
                  fn=mk_hmul(hb, 0), fence="once")
            w = ["sadd_e15", "qkev15"] if hb == 0 else [f"silu1_{hb-1}"]
            em.op("tensor", f"tGU_g1_{hb}", waits=w, fn=mk_g(hb, 1, ("PB4", "PB5")))
            em.op("tensor", f"tGU_u1_{hb}", waits=[f"hmul0_{hb}"], fn=mk_u(hb, 1))
            em.op("scalar", f"silu1_{hb}", waits=[f"tGU_g1_{hb}"],
                  fn=mk_silu(hb, 1, ("PB4", "PB5")), fence="once")
            em.op("vector", f"hmul1_{hb}", waits=[f"silu1_{hb}", f"tGU_u1_{hb}"],
                  fn=mk_hmul(hb, 1), fence="once")
            if hb >= 18:
                for hk in (2 * (hb - 18), 2 * (hb - 18) + 1):
                    def f_wd(e, hk=hk):
                        e.dma_start(tb["wdTA"][:, 1024 * hk:1024 * hk + 1024],
                                    tb["d_wdt"][hk]).then_inc(em.sems["wdA"], 16)
                    em.op("pool", f"wd_dma{hk}", fn=f_wd, incs=16, self_inc=True,
                          sem="wdA" if hk < 12 else "wdB")

    def p3(em):
        for hk in range(12, NHB):
            def f_wd(e, hk=hk):
                e.dma_start(tb["wdTB"][:, 1024 * (hk - 12):1024 * (hk - 12) + 1024],
                            tb["d_wdt"][hk]).then_inc(em.sems["wdB"], 16)
            em.op("pool", f"wd_dma{hk}", fn=f_wd, incs=16, self_inc=True, sem="wdB")

        for c in range(NCK):
            def f_x2(e, c=c):
                e.dma_start(tb["x2"][:, 1024 * (c % 2):1024 * (c % 2) + 1024],
                            tb["d_x"][128 * c:128 * c + 128, :]).then_inc(em.sems[f"ox{c % 2}"], 16)
            w = [f"oadd{c-2}"] if c >= 2 else ["tGU_u1_23"]
            em.op("sync", f"x2dma{c}", waits=w, fn=f_x2, incs=16, self_inc=True, sem=f"ox{c % 2}")

            for sub in range(2):
                def f_dn(e, c=c, sub=sub):
                    pb = ("PB0", "PB1") if c % 2 == 0 else ("PB2", "PB3")
                    last = None
                    for hk in range(12 * sub, 12 * sub + 12):
                        lhs = tb["hT"][:, 2048 * hk + 128 * c:2048 * hk + 128 * c + 128]
                        wdt = tb["wdTA"] if hk < 12 else tb["wdTB"]
                        col = 1024 * (hk % 12)
                        for n in range(2):
                            last = e.matmul(tb[pb[n]][:, 0:512], lhs,
                                            wdt[:, col + 512 * n:col + 512 * n + 512],
                                            start=hk == 0, stop=hk == 23)
                    return last
                w = [f"wd_dma{12 * sub + 11}"]
                if sub == 0:
                    if c == 0:
                        w += ["hmul1_23", "silu0_23"]
                    if c >= 2:
                        w += [f"g{c-2}"]
                em.op("tensor", f"dn{sub}_{c}", waits=w, fn=f_dn)

            def f_g(e, c=c):
                pb = ("PB0", "PB1") if c % 2 == 0 else ("PB2", "PB3")
                ga = tb["g_st0"] if c % 2 == 0 else tb["g_st1"]
                e.activation(ga[:, 0:512], tb[pb[0]][:], AF.Silu, scale=1.0 / 16129.0)
                return e.activation(ga[:, 512:1024], tb[pb[1]][:], AF.Silu,
                                    scale=1.0 / 16129.0)
            w = [f"dn1_{c}"] + ([f"odma{c-2}"] if c >= 2 else [])
            em.op("scalar", f"g{c}", waits=w, fn=f_g, fence="once")

            def f_sg(e, c=c):
                ga = tb["g_st0"] if c % 2 == 0 else tb["g_st1"]
                return e.tensor_tensor(ga[:], ga[:], tb["s_sb"][:, 1024 * c:1024 * c + 1024],
                                       ALU.mult)
            em.op("vector", f"sg{c}", waits=[f"g{c}"], fn=f_sg, fence="once")

            def f_oadd(e, c=c):
                ga = tb["g_st0"] if c % 2 == 0 else tb["g_st1"]
                return e.tensor_tensor(ga[:], ga[:],
                                       tb["x2"][:, 1024 * (c % 2):1024 * (c % 2) + 1024],
                                       ALU.add)
            em.op("pool", f"oadd{c}", waits=[f"sg{c}", f"x2dma{c}"], fn=f_oadd, fence="once")

            def f_od(e, c=c):
                ga = tb["g_st0"] if c % 2 == 0 else tb["g_st1"]
                e.dma_start(tb["d_out"][128 * c:128 * c + 128, :],
                            ga[:]).then_inc(em.sems[f"ou{c % 2}"], 16)
            em.op("sync", f"odma{c}", waits=[f"oadd{c}"], fn=f_od, incs=16, self_inc=True, sem=f"ou{c % 2}")

        em.wait_only("sync", "fin", [f"odma{c}" for c in range(NCK)])

    return p0, attn, p2, p3


def _dbg_taps(em, tb, names):
    for nm, waits in names:
        def f(e, nm=nm):
            e.dma_start(tb["d_" + nm], tb[nm][:]).then_inc(em.sems["tap"], 16)
        em.op("sync", f"tap_{nm}", waits=waits, fn=f, incs=16, self_inc=True, sem="tap")
    if names:
        em.wait_only("sync", "tapfin", [f"tap_{nm}" for nm, _ in names])


_CACHED = {}


def _build(debug=False, phases=4, ncks=NCK, astages=9999):
    if (debug, phases, ncks, astages) in _CACHED:
        return _CACHED[(debug, phases, ncks, astages)]
    nc = bass.Bass("TRN2", target_bir_lowering=False)
    tb = {}
    tb["d_x"] = nc.dram_tensor("x", [T, D], F32, kind="ExternalInput").ap()
    tb["d_wqt"] = nc.dram_tensor("wqt", [128, 2048], F32, kind="ExternalInput").ap()
    tb["d_wkt"] = nc.dram_tensor("wkt", [128, 2048], F32, kind="ExternalInput").ap()
    tb["d_wvt"] = nc.dram_tensor("wvt", [128, 8192], F32, kind="ExternalInput").ap()
    tb["d_wot"] = nc.dram_tensor("wot", [128, 8192], F32, kind="ExternalInput").ap()
    tb["d_wgt"] = nc.dram_tensor("wgt", [NHB, 128, 1024], I32, kind="ExternalInput").ap()
    tb["d_wut"] = nc.dram_tensor("wut", [NHB, 128, 1024], I32, kind="ExternalInput").ap()
    tb["d_wdt"] = nc.dram_tensor("wdt", [NHB, 128, 1024], I32, kind="ExternalInput").ap()
    tb["d_gpad"] = nc.dram_tensor("gpad", [128, 512], F32, kind="ExternalInput").ap()
    tb["d_bpad"] = nc.dram_tensor("bpad", [128, 512], F32, kind="ExternalInput").ap()
    tb["d_identb"] = nc.dram_tensor("identb", [128, 128], BF16, kind="ExternalInput").ap()
    tb["d_maskb"] = nc.dram_tensor("maskb", [128, 128], BF16, kind="ExternalInput").ap()
    tb["d_out"] = nc.dram_tensor("out", [T, D], F32, kind="ExternalOutput").ap()
    dbg_attn = []
    dbg_p3 = []
    if debug:
        for nm, shp, dt in (("xnT", [128, 16384], BF16), ("s_sb", [128, 16384], BF16),
                            ("S_f32", [128, 2080], F32), ("S_bf", [128, 2080], BF16),
                            ("phik", [128, 8192], BF16), ("phiq", [128, 4096], BF16),
                            ("phiqT", [128, 4096], BF16), ("y_nat", [128, 1024], BF16),
                            ("qhat", [128, 2048], BF16), ("khd", [128, 256], BF16),
                            ("v_bf", [128, 2080], BF16), ("Asq", [128, 2048], BF16),
                            ("qhatT", [128, 2048], BF16), ("yT", [128, 1024], BF16)):
            tb["d_" + nm] = nc.dram_tensor("tap_" + nm, shp, dt, kind="ExternalOutput").ap()
        tb["d_hT"] = nc.dram_tensor("tap_hT", [128, 49152], BF16, kind="ExternalOutput").ap()
        last_attn = ["sevac15", "sbf15", "y15", "phitev2_15", "amask15",
                     "qkev15", "gb15", "vev2_15", "phiq15", "phik15"]
        dbg_attn = [(nm, last_attn) for nm in ("xnT", "s_sb", "S_f32", "S_bf", "phik",
                                               "phiq", "phiqT", "y_nat", "qhat", "khd",
                                               "v_bf", "Asq", "qhatT", "yT")]
        dbg_p3 = [("hT", ["hmul1_23"])]

    em = Em()
    p0f, attnf, p2f, p3f = _program(em, tb, debug, ncks=ncks, astages=astages)
    em.mode = "count"
    p0f(em)
    if phases >= 2:
        attnf(em)
        if debug:
            _dbg_taps(em, tb, dbg_attn)
    if phases >= 3:
        p2f(em)
        if debug:
            _dbg_taps(em, tb, dbg_p3)
    if phases >= 4:
        p3f(em)

    ctxs = []

    def enter(ctx):
        ctxs.append(ctx)
        return ctx.__enter__()

    sems = {}
    for e in SEM_NAMES:
        sems[e] = enter(nc.semaphore(f"sem_{e}"))
    em.sems = sems
    for i in range(6):
        tb[f"PB{i}"] = enter(nc.psum_tensor(f"pb{i}", [128, 512], F32))
    for i in (6, 7):
        tb[f"PB{i}"] = enter(nc.psum_tensor(f"pb{i}", [128, 1024], BF16))

    tb["s_sb"] = enter(nc.sbuf_tensor("s_sb", [128, 16384], BF16))
    tb["identb"] = enter(nc.sbuf_tensor("identb_sb", [128, 128], BF16))
    tb["maskb"] = enter(nc.sbuf_tensor("maskb_sb", [128, 128], BF16))
    tb["gpad"] = enter(nc.sbuf_tensor("gpad_sb", [128, 512], F32))
    tb["bpad"] = enter(nc.sbuf_tensor("bpad_sb", [128, 512], F32))
    tb["eps_sb"] = enter(nc.sbuf_tensor("eps_sb", [128, 1], F32))
    ctx_wg = nc.sbuf_tensor("wg_bf", [128, 2048], BF16)
    tb["wg_bf"] = ctx_wg.__enter__()
    ctx_wu = nc.sbuf_tensor("wu_bf", [128, 2048], BF16)
    tb["wu_bf"] = ctx_wu.__enter__()
    ctx_xnt = nc.sbuf_tensor("xnT", [128, 16384], BF16)
    tb["xnT"] = ctx_xnt.__enter__()

    def run_block(progfn, extra_dbg=None):
        with nc.Block() as blk:
            for ename in ("sync", "tensor", "vector", "scalar", "pool"):
                def body(eng_obj, ename=ename):
                    em.mode = "emit"
                    em.emit_eng = ename
                    em.obj = eng_obj
                    progfn(em)
                    if extra_dbg:
                        _dbg_taps(em, tb, extra_dbg)
                getattr(blk, "gpsimd" if ename == "pool" else ename)(body)

    aw_ctxs = []

    def enterAW(ctx):
        aw_ctxs.append(ctx)
        return ctx.__enter__()

    tb["wq_bf"] = enterAW(nc.sbuf_tensor("wq_bf", [128, 2048], BF16))
    tb["wk_bf"] = enterAW(nc.sbuf_tensor("wk_bf", [128, 2048], BF16))
    tb["wv_bf"] = enterAW(nc.sbuf_tensor("wv_bf", [128, 8192], BF16))
    tb["wo_bf"] = enterAW(nc.sbuf_tensor("wo_bf", [128, 8192], BF16))

    p0_ctxs = []

    def enter0(ctx):
        p0_ctxs.append(ctx)
        return ctx.__enter__()

    tb["x_t"] = enter0(nc.sbuf_tensor("x_t", [128, 4096], F32))
    tb["xsq"] = enter0(nc.sbuf_tensor("xsq", [128, 2048], F32))
    tb["xn_nat"] = enter0(nc.sbuf_tensor("xn_nat", [128, 2048], BF16))
    tb["p0st"] = enter0(nc.sbuf_tensor("p0st", [128, 16], F32))
    run_block(p0f)
    for c in reversed(p0_ctxs):
        c.__exit__(None, None, None)

    a_ctxs = []

    def enterA(ctx):
        a_ctxs.append(ctx)
        return ctx.__enter__()

    tb["qsq"] = enterA(nc.sbuf_tensor("qsq", [128, 512], F32))
    tb["fst"] = enterA(nc.sbuf_tensor("fst", [128, 256], F32))
    tb["qc"] = enterA(nc.sbuf_tensor("qc", [128, 512], F32))
    tb["qhat"] = enterA(nc.sbuf_tensor("qhat", [128, 2048], BF16))
    tb["khat2"] = enterA(nc.sbuf_tensor("khat2", [128, 512], BF16))
    tb["khd"] = enterA(nc.sbuf_tensor("khd", [128, 256], BF16))
    tb["qhatT"] = enterA(nc.sbuf_tensor("qhatT", [128, 2048], BF16))
    tb["khatT2"] = enterA(nc.sbuf_tensor("khatT2", [128, 512], BF16))
    tb["phiq"] = enterA(nc.sbuf_tensor("phiq", [128, 4096], BF16))
    tb["phik"] = enterA(nc.sbuf_tensor("phik", [128, 8192], BF16))
    tb["phiqT"] = enterA(nc.sbuf_tensor("phiqT", [128, 4096], BF16))
    tb["Asq"] = enterA(nc.sbuf_tensor("Asq", [128, 2048], BF16))
    tb["v_bf"] = enterA(nc.sbuf_tensor("v_bf", [128, 2080], BF16))
    tb["qk_sb"] = enterA(nc.sbuf_tensor("qk_sb", [128, 512], F32))
    tb["S_f32"] = enterA(nc.sbuf_tensor("S_f32", [128, 2080], F32))
    tb["S_bf"] = enterA(nc.sbuf_tensor("S_bf", [128, 2080], BF16))
    tb["y_nat"] = enterA(nc.sbuf_tensor("y_nat", [128, 1024], BF16))
    tb["yT"] = enterA(nc.sbuf_tensor("yT", [128, 1024], BF16))
    tb["den"] = enterA(nc.sbuf_tensor("den", [128, 32], F32))
    if phases >= 2:
        run_block(attnf, extra_dbg=dbg_attn if debug else None)
    for c in reversed(a_ctxs):
        c.__exit__(None, None, None)
    for c in reversed(aw_ctxs):
        c.__exit__(None, None, None)

    ctx_ht = nc.sbuf_tensor("hT", [128, 49152], BF16, side="right")
    tb["hT"] = ctx_ht.__enter__()
    ctx_wdta = nc.sbuf_tensor("wdTA", [128, 12288], BF16, side="right")
    tb["wdTA"] = ctx_wdta.__enter__()
    ctx_ga0 = nc.sbuf_tensor("g_act0", [128, 1024], F32)
    tb["g_act0"] = ctx_ga0.__enter__()
    ctx_ga1 = nc.sbuf_tensor("g_act1", [128, 1024], F32)
    tb["g_act1"] = ctx_ga1.__enter__()
    if phases >= 3:
        run_block(p2f, extra_dbg=dbg_p3 if debug else None)
    ctx_ga1.__exit__(None, None, None)
    ctx_ga0.__exit__(None, None, None)
    ctx_xnt.__exit__(None, None, None)
    ctx_wu.__exit__(None, None, None)
    ctx_wg.__exit__(None, None, None)

    c_ctxs = []

    def enterC(ctx):
        c_ctxs.append(ctx)
        return ctx.__enter__()

    tb["wdTB"] = enterC(nc.sbuf_tensor("wdTB", [128, 12288], BF16, side="right"))
    tb["g_st0"] = enterC(nc.sbuf_tensor("g_st0", [128, 1024], F32))
    tb["g_st1"] = enterC(nc.sbuf_tensor("g_st1", [128, 1024], F32))
    tb["x2"] = enterC(nc.sbuf_tensor("x2", [128, 2048], F32))
    if phases >= 4:
        run_block(p3f)
    for c in reversed(c_ctxs):
        c.__exit__(None, None, None)
    ctx_wdta.__exit__(None, None, None)
    ctx_ht.__exit__(None, None, None)
    for c in reversed(ctxs):
        c.__exit__(None, None, None)

    _CACHED[(debug,)] = nc
    return nc


def _host_inputs(Wq, Wk, Wv, Wo, gamma, beta, wg, wu, wd):
    f32 = np.float32
    def att_w(w, fcols):
        return np.ascontiguousarray(
            np.asarray(w, f32).T.reshape(8, 128, fcols).transpose(1, 0, 2).reshape(128, 8 * fcols))
    wqt = att_w(Wq, 256)
    wkt = att_w(Wk, 256)
    wvt = att_w(Wv, 1024)
    wot = att_w(Wo, 1024)
    wgt = np.ascontiguousarray(
        np.asarray(wg, np.int32).reshape(24, 128, 8, 128).transpose(0, 3, 2, 1).reshape(24, 128, 1024))
    wut = np.ascontiguousarray(
        np.asarray(wu, np.int32).reshape(24, 128, 8, 128).transpose(0, 3, 2, 1).reshape(24, 128, 1024))
    wdt = np.ascontiguousarray(np.asarray(wd, np.int32).T.reshape(24, 128, 1024))
    gpad = np.ascontiguousarray(
        np.broadcast_to(np.tile(np.asarray(gamma, f32), 32), (128, 512)))
    bpad = np.ascontiguousarray(
        np.broadcast_to(np.tile(np.asarray(beta, f32), 32), (128, 512)))
    identb = np.eye(128, dtype=BF)
    maskb = np.triu(np.ones((128, 128), f32)).astype(BF)
    return dict(wqt=wqt, wkt=wkt, wvt=wvt, wot=wot, wgt=wgt, wut=wut, wdt=wdt,
                gpad=gpad, bpad=bpad, identb=identb, maskb=maskb)


def kernel(L0_attn_hidden_states, L0_attn_Wq, L0_attn_Wk, L0_attn_Wv, L0_attn_Wo,
           L0_attn_gamma, L0_attn_beta, L0_ffn_w_gate, L0_ffn_w_up, L0_ffn_w_down):
    x = np.ascontiguousarray(np.asarray(L0_attn_hidden_states, np.float32))
    shared = _host_inputs(L0_attn_Wq, L0_attn_Wk, L0_attn_Wv, L0_attn_Wo,
                          L0_attn_gamma, L0_attn_beta,
                          L0_ffn_w_gate, L0_ffn_w_up, L0_ffn_w_down)
    nc = _build(debug=False)
    in_maps = [dict(shared, x=np.ascontiguousarray(x[i])) for i in range(8)]
    res = run_bass_kernel_spmd(nc, in_maps, core_ids=list(range(8)))
    return np.stack([res.results[i]["out"] for i in range(8)], axis=0)


# version 24
# speedup vs baseline: 1.0295x; 1.0295x over previous
import numpy as np
import ml_dtypes
import concourse.bass as bass
import concourse.mybir as mybir
from concourse.bass_utils import run_bass_kernel_spmd

F32 = mybir.dt.float32
BF16 = mybir.dt.bfloat16
I32 = mybir.dt.int32
AF = mybir.ActivationFunctionType
ALU = mybir.AluOpType
AX = mybir.AxisListType
BF = ml_dtypes.bfloat16

T = 2048
D = 1024
NH = 16
FT = 16
HD = 64
HID = 3072
NCK = 16
NHB = 24
EPS = 1e-5
C_OFF = float(np.sqrt(2.0) / 4.0)
OFFD = [0] * 16
for _d in range(1, 16):
    OFFD[_d] = 16 + sum(16 - dd for dd in range(1, _d))

REAL_ENGS = ("sync", "tensor", "vector", "scalar", "pool")
CHAIN_ENGS = {"vector": "vch", "scalar": "sch", "pool": "pch"}
EMIT_METHODS = frozenset("""tensor_tensor tensor_add tensor_sub tensor_mul tensor_max
 tensor_scalar tensor_single_scalar tensor_scalar_add tensor_scalar_sub
 tensor_scalar_mul tensor_scalar_max tensor_scalar_min tensor_copy tensor_reduce
 reduce_sum reduce_max scalar_tensor_tensor tensor_tensor_scan memset memzero
 reciprocal bn_stats bn_aggr select transpose iota activation copy mul add square
 sqrt sign engine_nop stream_shuffle pool pool_avg pool_max tensor_relu""".split())


class Fenced:

    def __init__(self, obj, sem, holder, mode="all"):
        self._o = obj
        self._sem = sem
        self._h = holder
        self._mode = mode
        self._first = True
        self.last_inst = None

    def __getattr__(self, name):
        f = getattr(self._o, name)
        if name not in EMIT_METHODS:
            return f
        def wrap(*a, **kw):
            if self._h[0] > 0 and (self._mode == "all" or self._first):
                self._o.wait_ge(self._sem, self._h[0])
            self._first = False
            inst = f(*a, **kw)
            if self._mode == "all":
                inst.then_inc(self._sem, 1)
                self._h[0] += 1
            self.last_inst = inst
            return inst
        return wrap
SEM_NAMES = list(REAL_ENGS) + ["c", "x0", "x1", "x2", "x3", "wa", "gu0", "gu1",
                               "wdA", "wdB", "ox0", "ox1", "ou0", "ou1", "tap",
                               "vch", "sch", "pch"]


class Em:

    def __init__(self):
        self.mode = "count"
        self.cnt = {e: 0 for e in SEM_NAMES}
        self.val = {}
        self.key_eng = {}
        self.emit_eng = None
        self.obj = None
        self.sems = None
        self.waited = {e: {} for e in REAL_ENGS}
        self.chain = {e: [0] for e in CHAIN_ENGS}

    def op(self, eng, key, waits=(), fn=None, incs=1, self_inc=False, sem=None,
           fence="all"):
        sem = sem or eng
        if self.mode == "count":
            for w in waits:
                assert w in self.val, f"wait on unknown key {w} (from {key})"
            self.cnt[sem] += incs
            assert key not in self.val, f"dup key {key}"
            self.val[key] = (sem, self.cnt[sem])
            self.key_eng[key] = eng
            return
        if eng != self.emit_eng:
            return
        wd = self.waited[self.emit_eng]
        for w in waits:
            src, v = self.val[w]
            if self.key_eng[w] == eng and src in REAL_ENGS:
                continue
            if wd.get(src, -1) >= v:
                continue
            self.obj.wait_ge(self.sems[src], v)
            wd[src] = v
        if fn is not None:
            if eng in CHAIN_ENGS:
                ch_sem = self.sems[CHAIN_ENGS[eng]]
                holder = self.chain[eng]
                fe = Fenced(self.obj, ch_sem, holder, mode=fence)
                last = fn(fe)
                if not self_inc:
                    assert last is not None, f"fn for {key} returned no instruction"
                    if fence != "all" and fe.last_inst is not None:
                        fe.last_inst.then_inc(ch_sem, 1)
                        holder[0] += 1
                    self.obj.wait_ge(ch_sem, holder[0])
                    self.obj.nop().then_inc(self.sems[sem], incs)
            else:
                last = fn(self.obj)
                if not self_inc:
                    assert last is not None, f"fn for {key} returned no instruction"
                    last.then_inc(self.sems[sem], incs)

    def wait_only(self, eng, key, waits):
        if self.mode == "count":
            for w in waits:
                assert w in self.val
            return
        if eng != self.emit_eng:
            return
        wd = self.waited[self.emit_eng]
        for w in waits:
            src, v = self.val[w]
            if wd.get(src, -1) >= v:
                continue
            self.obj.wait_ge(self.sems[src], v)
            wd[src] = v


def _program(em, tb, dbg, ncks=NCK, astages=9999):

    def ap(name, off, dims):
        return bass.AP(tb[name], off, dims)

    def p0(em):
        def f_const(e):
            e.dma_start(tb["identb"][:], tb["d_identb"]).then_inc(em.sems["c"], 16)
            e.dma_start(tb["maskb"][:], tb["d_maskb"]).then_inc(em.sems["c"], 16)
            e.dma_start(tb["gpad"][:], tb["d_gpad"]).then_inc(em.sems["c"], 16)
            e.dma_start(tb["bpad"][:], tb["d_bpad"]).then_inc(em.sems["c"], 16)
        em.op("sync", "cdma", fn=f_const, incs=64, self_inc=True, sem="c")

        def f_init(e):
            return e.memset(tb["eps_sb"][:], EPS)
        em.op("pool", "init", fn=f_init)

        def f_init2p(e):
            e.memset(tb["S_f32"][:], 0.0)
            e.memset(tb["S_bf"][:], 0.0)
            e.memset(tb["phiqT"][:], 0.0)
            e.memset(ap("v_bf", 64, [[2080, 128], [65, 16]]), 1.0)
            e.memset(ap("v_bf", 1040 + 64, [[2080, 128], [65, 16]]), 1.0)
            e.tensor_scalar_mul(tb["gpad"][:], tb["gpad"][:], 0.5)
            return e.tensor_scalar_mul(tb["bpad"][:], tb["bpad"][:], 0.5)
        em.op("pool", "init2p", waits=["cdma"], fn=f_init2p)

        def f_wdma(e):
            e.dma_start(tb["wq_bf"][:], tb["d_wqt"]).then_inc(em.sems["wa"], 16)
            e.dma_start(tb["wk_bf"][:], tb["d_wkt"]).then_inc(em.sems["wa"], 16)
            e.dma_start(tb["wv_bf"][:], tb["d_wvt"]).then_inc(em.sems["wa"], 16)
            e.dma_start(tb["wo_bf"][:], tb["d_wot"]).then_inc(em.sems["wa"], 16)
        em.op("pool", "wdma", fn=f_wdma, incs=64, self_inc=True, sem="wa")
        for hb in (0, 1):
            def f_gu(e, hb=hb):
                par = hb % 2
                e.dma_start(tb["wg_bf"][:, 1024 * par:1024 * par + 1024],
                            tb["d_wgt"][hb]).then_inc(em.sems[f"gu{hb % 2}"], 16)
                e.dma_start(tb["wu_bf"][:, 1024 * par:1024 * par + 1024],
                            tb["d_wut"][hb]).then_inc(em.sems[f"gu{hb % 2}"], 16)
            em.op("pool", f"gu_dma{hb}", fn=f_gu, incs=32, self_inc=True, sem=f"gu{hb % 2}")

        def mk_xdma(c):
            def f(e):
                e.dma_start(tb["x_t"][:, 1024 * (c % 4):1024 * (c % 4) + 1024],
                            tb["d_x"][128 * c:128 * c + 128, :]).then_inc(em.sems[f"x{c % 4}"], 16)
            return f

        def mk_p0sq(c):
            def f(e):
                return e.activation(tb["xsq"][:, 1024 * (c % 2):1024 * (c % 2) + 1024],
                                    tb["x_t"][:, 1024 * (c % 4):1024 * (c % 4) + 1024],
                                    AF.Square)
            return f

        def mk_p0stat(c):
            par = 8 * (c % 2)
            def f(e):
                xt = tb["x_t"][:, 1024 * (c % 4):1024 * (c % 4) + 1024]
                st = tb["p0st"]
                e.reduce_sum(st[:, par + 0:par + 1], xt, axis=AX.X)
                e.reduce_sum(st[:, par + 1:par + 2],
                             tb["xsq"][:, 1024 * (c % 2):1024 * (c % 2) + 1024], axis=AX.X)
                e.tensor_scalar_mul(st[:, par + 2:par + 3], st[:, par + 0:par + 1], 1.0 / D)
                e.tensor_mul(st[:, par + 3:par + 4], st[:, par + 2:par + 3],
                             st[:, par + 2:par + 3])
                return e.scalar_tensor_tensor(st[:, par + 4:par + 5], st[:, par + 1:par + 2],
                                              1.0 / D, st[:, par + 3:par + 4],
                                              ALU.mult, ALU.subtract)
            return f

        def mk_p0rs(c):
            par = 8 * (c % 2)
            def f(e):
                return e.activation(tb["p0st"][:, par + 5:par + 6], tb["p0st"][:, par + 4:par + 5],
                                    AF.Sqrt, bias=tb["eps_sb"][:, 0:1])
            return f

        def mk_p0xc(c):
            par = 8 * (c % 2)
            def f(e):
                st = tb["p0st"]
                e.reciprocal(st[:, par + 6:par + 7], st[:, par + 5:par + 6])
                return e.tensor_scalar(tb["xn_nat"][:, 1024 * (c % 2):1024 * (c % 2) + 1024],
                                       tb["x_t"][:, 1024 * (c % 4):1024 * (c % 4) + 1024],
                                       st[:, par + 2:par + 3], st[:, par + 6:par + 7],
                                       ALU.subtract, ALU.mult)
            return f

        def mk_p0t(c):
            def f(e):
                pb = (tb["PB0"], tb["PB1"]) if c % 2 == 0 else (tb["PB2"], tb["PB3"])
                last = None
                for dk in range(8):
                    last = e.matmul(pb[dk // 4][:, 128 * (dk % 4):128 * (dk % 4) + 128],
                                    tb["xn_nat"][:, 1024 * (c % 2) + 128 * dk:
                                                 1024 * (c % 2) + 128 * dk + 128],
                                    tb["identb"][:], start=True, stop=True)
                return last
            return f

        def mk_p0evac(c):
            def f(e):
                names = ("PB0", "PB1") if c % 2 == 0 else ("PB2", "PB3")
                last = None
                for i, pn in enumerate(names):
                    last = e.activation(
                        ap("xnT", 2048 * (4 * i) + 128 * c, [[16384, 128], [2048, 4], [1, 128]]),
                        tb[pn][:], AF.Copy)
                return last
            return f

        for i in range(NCK + 3):
            if i < NCK:
                w = [f"p0xc{i-4}"] if i >= 4 else []
                em.op("sync", f"xdma{i}", waits=w, fn=mk_xdma(i), incs=16,
                      self_inc=True, sem=f"x{i % 4}")
                w = [f"xdma{i}"] + ([f"p0stat{i-2}"] if i >= 2 else [])
                em.op("scalar", f"p0sq{i}", waits=w, fn=mk_p0sq(i), fence="once")
                em.op("vector", f"p0stat{i}", waits=[f"p0sq{i}"], fn=mk_p0stat(i))
            c1 = i - 1
            if 0 <= c1 < NCK:
                em.op("scalar", f"p0rs{c1}", waits=[f"p0stat{c1}", "init"], fn=mk_p0rs(c1),
                      fence="once")
                w = [f"p0rs{c1}"] + ([f"p0t{c1-2}"] if c1 >= 2 else [])
                em.op("vector", f"p0xc{c1}", waits=w, fn=mk_p0xc(c1))
                w = [f"p0xc{c1}", "cdma"] + ([f"p0evac{c1-2}"] if c1 >= 2 else [])
                em.op("tensor", f"p0t{c1}", waits=w, fn=mk_p0t(c1))
            c2 = i - 2
            if 0 <= c2 < NCK:
                em.op("scalar", f"p0evac{c2}", waits=[f"p0t{c2}"], fn=mk_p0evac(c2),
                      fence="once")

    def attn(em):
        _n = [0]
        _orig_op = em.op

        def _op(*a, **kw):
            _n[0] += 1
            if _n[0] > astages:
                return
            return _orig_op(*a, **kw)
        em = type("EmShim", (), {"op": staticmethod(_op), "sems": None,
                                 "wait_only": em.wait_only})()
        em.sems = _orig_op.__self__.sems

        def f_init2(e):
            e.memset(tb["qhat"][:], 0.0)
            last = e.memset(tb["khat2"][:], 0.0)
            if dbg:
                e.memset(tb["phiq"][:], 0.0)
                last = e.memset(tb["phik"][:], 0.0)
            return last
        em.op("vector", "init2", fn=f_init2)

        def mk_qkvA(c):
            def f(e):
                last = None
                for dk in range(8):
                    lhs = tb["xnT"][:, 2048 * dk + 128 * c:2048 * dk + 128 * c + 128]
                    e.matmul(tb["PB4"][:, 0:256], lhs,
                             tb["wq_bf"][:, 256 * dk:256 * dk + 256],
                             start=dk == 0, stop=dk == 7)
                    e.matmul(tb["PB5"][:, 0:256], lhs,
                             tb["wk_bf"][:, 256 * dk:256 * dk + 256],
                             start=dk == 0, stop=dk == 7)
                    last = e.matmul(tb["PB3"][:, 0:512], lhs,
                                    tb["wv_bf"][:, 1024 * dk:1024 * dk + 512],
                                    start=dk == 0, stop=dk == 7)
                return last
            return f

        def mk_vh2(c):
            def f(e):
                last = None
                for dk in range(8):
                    lhs = tb["xnT"][:, 2048 * dk + 128 * c:2048 * dk + 128 * c + 128]
                    last = e.matmul(tb["PB3"][:, 0:512], lhs,
                                    tb["wv_bf"][:, 1024 * dk + 512:1024 * dk + 1024],
                                    start=dk == 0, stop=dk == 7)
                return last
            return f

        def mk_vev(c, half):
            def f(e):
                return e.activation(
                    ap("v_bf", 1040 * (c % 2) + 65 * 8 * half,
                       [[2080, 128], [65, 8], [1, 64]]),
                    ap("PB3", 0, [[512, 128], [64, 8], [1, 64]]), AF.Copy)
            return f

        def mk_qkev(c):
            def f(e):
                e.activation(tb["qsq"][:, 0:256], tb["PB4"][:, 0:256], AF.Square)
                e.activation(tb["qsq"][:, 256:512], tb["PB5"][:, 0:256], AF.Square)
                e.activation(tb["qk_sb"][:, 0:256], tb["PB4"][:, 0:256], AF.Copy)
                return e.activation(tb["qk_sb"][:, 256:512], tb["PB5"][:, 0:256], AF.Copy)
            return f

        def mk_featln(c):
            def f(e):
                st = tb["fst"]
                e.reduce_sum(st[:, 0:32],
                             tb["qk_sb"][:].rearrange("p (a b) -> p a b", b=16), axis=AX.X)
                e.reduce_sum(st[:, 32:64],
                             tb["qsq"][:].rearrange("p (a b) -> p a b", b=16), axis=AX.X)
                e.tensor_scalar_mul(st[:, 64:96], st[:, 0:32], 1.0 / FT)
                e.tensor_mul(st[:, 96:128], st[:, 64:96], st[:, 64:96])
                return e.scalar_tensor_tensor(st[:, 128:160], st[:, 32:64], 1.0 / FT,
                                              st[:, 96:128], ALU.mult, ALU.subtract)
            return f

        def mk_rsq(c):
            def f(e):
                return e.activation(tb["fst"][:, 160:192], tb["fst"][:, 128:160], AF.Sqrt,
                                    bias=tb["eps_sb"][:, 0:1])
            return f

        def mk_qc(c):
            def f(e):
                st = tb["fst"]
                e.reciprocal(st[:, 192:224], st[:, 160:192])
                e.tensor_tensor(ap("qc", 0, [[512, 128], [16, 32], [1, 16]]),
                                ap("qk_sb", 0, [[512, 128], [16, 32], [1, 16]]),
                                ap("fst", 64, [[256, 128], [1, 32], [0, 16]]),
                                ALU.subtract)
                return e.tensor_tensor(ap("qc", 0, [[512, 128], [16, 32], [1, 16]]),
                                       ap("qc", 0, [[512, 128], [16, 32], [1, 16]]),
                                       ap("fst", 192, [[256, 128], [1, 32], [0, 16]]),
                                       ALU.mult)
            return f

        def mk_gb(c):
            def f(e):
                e.tensor_tensor(tb["qc"][:], tb["qc"][:], tb["gpad"][:], ALU.mult)
                e.tensor_tensor(ap("qhat", 0, [[2048, 128], [128, 16], [1, 16]]),
                                ap("qc", 0, [[512, 128], [16, 16], [1, 16]]),
                                ap("bpad", 0, [[512, 128], [16, 16], [1, 16]]),
                                ALU.add)
                e.tensor_tensor(ap("khat2", 0, [[512, 128], [128, 4], [32, 4], [1, 16]]),
                                ap("qc", 256, [[512, 128], [64, 4], [16, 4], [1, 16]]),
                                ap("bpad", 256, [[512, 128], [64, 4], [16, 4], [1, 16]]),
                                ALU.add)
                return e.tensor_tensor(tb["khd"][:], tb["qc"][:, 256:512],
                                       tb["bpad"][:, 256:512], ALU.add)
            return f

        def mk_phiq(c):
            def f(e):
                return e.tensor_tensor(
                    bass.AP(tb["phiq"], 0, [[4096, 128], [256, 16], [16, 16], [1, 16]]),
                    bass.AP(tb["qhat"], 0, [[2048, 128], [128, 16], [1, 16], [0, 16]]),
                    bass.AP(tb["qhat"], 0, [[2048, 128], [128, 16], [0, 16], [1, 16]]),
                    ALU.mult)
            return f

        def mk_phik(c):
            par = 4096 * (c % 2)
            def f(e):
                return e.tensor_tensor(
                    bass.AP(tb["phik"], par, [[8192, 128], [256, 16], [16, 16], [1, 16]]),
                    bass.AP(tb["khd"], 0, [[256, 128], [16, 16], [1, 16], [0, 16]]),
                    bass.AP(tb["khd"], 0, [[256, 128], [16, 16], [0, 16], [1, 16]]),
                    ALU.mult)
            return f

        def mk_trsq_a():
            def f(e):
                last = None
                for g in range(12):
                    if g < 8:
                        pb, col = tb["PB7"], 128 * g
                    else:
                        pb, col = tb["PB6"], 512 + 128 * (g - 8)
                    src = 128 * g - 32 * (g % 4)
                    last = e.transpose(pb[:, col:col + 128],
                                       tb["qhat"][:, src:src + 128],
                                       tb["identb"][:])
                return last
            return f

        def mk_trsq_b():
            def f(e):
                last = None
                for g in range(12, 16):
                    src = 128 * g - 32 * (g % 4)
                    last = e.transpose(tb["PB6"][:, 128 * (g - 12):128 * (g - 12) + 128],
                                       tb["qhat"][:, src:src + 128],
                                       tb["identb"][:])
                return last
            return f

        def mk_trsevq():
            def f(e):
                e.activation(tb["qhatT"][:, 0:1024], tb["PB7"][:], AF.Copy)
                e.activation(tb["qhatT"][:, 1024:1536], tb["PB6"][:, 512:1024], AF.Copy)
                return e.activation(tb["qhatT"][:, 1536:2048], tb["PB6"][:, 0:512],
                                    AF.Copy)
            return f

        def mk_trsk2():
            def f(e):
                last = None
                for a in range(4):
                    last = e.transpose(tb["PB6"][:, 128 * a:128 * a + 128],
                                       tb["khat2"][:, 128 * a:128 * a + 128],
                                       tb["identb"][:])
                return last
            return f

        def mk_trskev():
            def f(e):
                return e.tensor_copy(tb["khatT2"][:], tb["PB6"][:, 0:512])
            return f

        AB = ("PB0", "PB1", "PB2")

        def mk_amm1(c):
            def f(e):
                last = None
                for a in range(3):
                    last = e.matmul(tb[AB[a]][:, 0:512],
                                    tb["khatT2"][:, 128 * a:128 * a + 128],
                                    tb["qhatT"][:, 512 * a:512 * a + 512],
                                    start=True, stop=True)
                return last
            return f

        def mk_amm2(c):
            def f(e):
                return e.matmul(tb["PB0"][:, 0:512],
                                tb["khatT2"][:, 384:512],
                                tb["qhatT"][:, 1536:2048],
                                start=True, stop=True)
            return f

        def mk_asq1(c):
            def f(e):
                last = None
                for i, pbn in enumerate(AB):
                    last = e.activation(tb["Asq"][:, 512 * i:512 * i + 512],
                                        tb[pbn][:], AF.Square)
                return last
            return f

        def mk_asq2(c):
            def f(e):
                return e.activation(tb["Asq"][:, 1536:2048], tb["PB0"][:],
                                    AF.Square)
            return f

        def mk_amask(c):
            def f(e):
                return e.tensor_tensor(
                    tb["Asq"][:].rearrange("p (h t) -> p h t", t=128),
                    tb["Asq"][:].rearrange("p (h t) -> p h t", t=128),
                    bass.AP(tb["maskb"], 0, [[128, 128], [0, 16], [1, 128]]),
                    ALU.mult)
            return f

        def mk_phit1(c):
            def f(e):
                last = None
                for b in range(16):
                    pb = tb["PB6"] if b < 8 else tb["PB7"]
                    last = e.transpose(pb[:, 128 * (b % 8):128 * (b % 8) + 128],
                                       tb["phiq"][:, 128 * b:128 * b + 128],
                                       tb["identb"][:])
                return last
            return f

        def mk_phitev1(c):
            def f(e):
                e.activation(tb["phiqT"][:, 0:1024], tb["PB6"][:], AF.Copy)
                return e.activation(tb["phiqT"][:, 1024:2048], tb["PB7"][:], AF.Copy)
            return f

        def mk_phit2(c):
            def f(e):
                last = None
                for b in range(16, 32):
                    pb = tb["PB6"] if b < 24 else tb["PB7"]
                    last = e.transpose(pb[:, 128 * (b % 8):128 * (b % 8) + 128],
                                       tb["phiq"][:, 128 * b:128 * b + 128],
                                       tb["identb"][:])
                return last
            return f

        def mk_phitev2(c):
            def f(e):
                e.activation(tb["phiqT"][:, 2048:3072], tb["PB6"][:], AF.Copy)
                return e.activation(tb["phiqT"][:, 3072:4096], tb["PB7"][:], AF.Copy)
            return f

        def numreg(h):
            if h < 7:
                return "PB0", 65 * h
            if h < 14:
                return "PB1", 65 * (h - 7)
            return "PB2", 65 * (h - 14)

        def mk_num(c):
            def f(e):
                last = None
                vb = 1040 * (c % 2)
                for h in range(16):
                    pbn, col = numreg(h)
                    reg = tb[pbn][:, col:col + 65]
                    e.matmul(reg, tb["Asq"][:, 128 * h:128 * h + 128],
                             ap("v_bf", vb + 65 * h, [[2080, 128], [1, 65]]),
                             start=True, stop=False)
                    e.matmul(reg, tb["phiqT"][:, 256 * h:256 * h + 128],
                             tb["S_bf"][:, 130 * h:130 * h + 65], start=False, stop=False)
                    last = e.matmul(reg, tb["phiqT"][:, 256 * h + 128:256 * h + 256],
                                    tb["S_bf"][:, 130 * h + 65:130 * h + 130],
                                    start=False, stop=True)
                return last
            return f

        def mk_y(c):
            def f(e):
                last = None
                for bi, (pbn, h0, nh) in enumerate((("PB0", 0, 7), ("PB1", 7, 7),
                                                    ("PB2", 14, 2))):
                    e.tensor_scalar_add(tb["den"][:, h0:h0 + nh],
                                        bass.AP(tb[pbn], 64, [[512, 128], [65, nh]]), EPS)
                    e.reciprocal(tb["den"][:, 16 + h0:16 + h0 + nh],
                                 tb["den"][:, h0:h0 + nh])
                    last = e.tensor_tensor(
                        ap("y_nat", 64 * h0, [[1024, 128], [64, nh], [1, 64]]),
                        bass.AP(tb[pbn], 0, [[512, 128], [65, nh], [1, 64]]),
                        ap("den", 16 + h0, [[32, 128], [1, nh], [0, 64]]),
                        ALU.mult)
                return last
            return f

        def smm_reg(e, c, r, col, pbn):
            par = 4096 * (c % 2)
            vb = 1040 * (c % 2)
            return e.matmul(tb[pbn][:, col:col + 65],
                            tb["phik"][:, par + 128 * r:par + 128 * r + 128],
                            ap("v_bf", vb + 65 * (r // 2), [[2080, 128], [1, 65]]),
                            start=True, stop=True)

        SMM_WAVES = ((0, 7), (7, 7), (14, 7), (21, 7), (28, 4))
        SMM_BANKS = ("PB4", "PB3", "PB4", "PB2", "PB4")

        def mk_smm(c, w):
            r0, nr = SMM_WAVES[w]
            def f(e):
                last = None
                for i in range(nr):
                    last = smm_reg(e, c, r0 + i, 65 * i, SMM_BANKS[w])
                return last
            return f

        def mk_sadd(c, w):
            r0, nr = SMM_WAVES[w]
            def f(e):
                sl = tb["S_f32"][:, 65 * r0:65 * (r0 + nr)]
                return e.tensor_tensor(sl, tb[SMM_BANKS[w]][:, 0:65 * nr], sl, ALU.add)
            return f

        def mk_sbf(c):
            def f(e):
                e.activation(tb["S_bf"][:, 0:1040], tb["S_f32"][:, 0:1040], AF.Copy)
                return e.activation(tb["S_bf"][:, 1040:2080], tb["S_f32"][:, 1040:2080],
                                    AF.Copy)
            return f

        def mk_yt(c):
            def f(e):
                last = None
                for i in range(8):
                    last = e.transpose(tb["PB7"][:, 128 * i:128 * i + 128],
                                       tb["y_nat"][:, 128 * i:128 * i + 128],
                                       tb["identb"][:])
                return last
            return f

        def mk_ytev(c):
            def f(e):
                return e.activation(tb["yT"][:], tb["PB7"][:], AF.Copy)
            return f

        def mk_wo(c):
            def f(e):
                last = None
                for yk in range(8):
                    lhs = tb["yT"][:, 128 * yk:128 * yk + 128]
                    e.matmul(tb["PB0"][:, 0:512], lhs,
                             tb["wo_bf"][:, 1024 * yk:1024 * yk + 512],
                             start=yk == 0, stop=yk == 7)
                    last = e.matmul(tb["PB1"][:, 0:512], lhs,
                                    tb["wo_bf"][:, 1024 * yk + 512:1024 * yk + 1024],
                                    start=yk == 0, stop=yk == 7)
                return last
            return f

        def mk_sevac(c):
            def f(e):
                e.activation(tb["s_sb"][:, 1024 * c:1024 * c + 512], tb["PB0"][:], AF.Copy)
                return e.activation(tb["s_sb"][:, 1024 * c + 512:1024 * c + 1024],
                                    tb["PB1"][:], AF.Copy)
            return f

        em.op("tensor", "qkvA0", waits=["p0evac15", "wdma"], fn=mk_qkvA(0))
        em.op("scalar", "vev1_0", waits=["qkvA0", "init2p"], fn=mk_vev(0, 0), fence="once")
        em.op("scalar", "qkev0", waits=["qkvA0"], fn=mk_qkev(0), fence="once")
        em.op("tensor", "vh2_0", waits=["vev1_0"], fn=mk_vh2(0))
        em.op("scalar", "vev2_0", waits=["vh2_0"], fn=mk_vev(0, 1), fence="once")
        em.op("vector", "featln0", waits=["qkev0"], fn=mk_featln(0))
        em.op("scalar", "rsq0", waits=["featln0", "init"], fn=mk_rsq(0), fence="once")
        em.op("vector", "qc0", waits=["rsq0"], fn=mk_qc(0))
        em.op("pool", "gb0", waits=["qc0", "init2", "init2p"], fn=mk_gb(0))
        em.op("vector", "phiq0", waits=["gb0", "init2"], fn=mk_phiq(0), fence="once")
        em.op("pool", "phik0", waits=["gb0", "init2"], fn=mk_phik(0), fence="once")

        WN = ("a", "b", "c", "d", "e")
        for c in range(ncks):
            nxt = c + 1 < ncks
            if c > 0:
                em.op("tensor", f"smm_e{c-1}", waits=[f"sadd_c{c-1}"], fn=mk_smm(c - 1, 4))
                em.op("vector", f"sadd_e{c-1}", waits=[f"smm_e{c-1}"], fn=mk_sadd(c - 1, 4),
                      fence="once")
                em.op("scalar", f"sbf{c-1}", waits=[f"sadd_e{c-1}"], fn=mk_sbf(c - 1),
                      fence="once")

            w = [f"gb{c}"]
            if c > 0:
                w += [f"phitev2_{c-1}", f"ytev{c-1}"]
            em.op("tensor", f"trsk{c}", waits=w, fn=mk_trsk2())
            em.op("vector", f"trsev{c}", waits=[f"trsk{c}"], fn=mk_trskev(),
                  fence="once")
            em.op("tensor", f"trsq_a{c}", waits=[f"gb{c}"], fn=mk_trsq_a())
            em.op("tensor", f"trsq_b{c}", waits=[f"trsev{c}"], fn=mk_trsq_b())
            em.op("scalar", f"trsevq{c}", waits=[f"trsq_b{c}"], fn=mk_trsevq(),
                  fence="once")
            if nxt:
                w2 = [f"qkev{c}", f"vev2_{c}"]
                if c > 0:
                    w2 += [f"sadd_e{c-1}", f"sadd_b{c-1}"]
                em.op("tensor", f"qkvA{c+1}", waits=w2, fn=mk_qkvA(c + 1))
                em.op("scalar", f"qkev{c+1}", waits=[f"qkvA{c+1}"], fn=mk_qkev(c + 1),
                      fence="once")

            w = [f"trsevq{c}", f"trsev{c}"]
            if c > 0:
                w += [f"sevac{c-1}", f"sadd_d{c-1}"]
            else:
                w += ["p0evac15", f"qkev{c}"]
            em.op("tensor", f"amm1_{c}", waits=w, fn=mk_amm1(c))
            em.op("scalar", f"asq1_{c}", waits=[f"amm1_{c}"], fn=mk_asq1(c), fence="once")
            if nxt:
                em.op("vector", f"featln{c+1}", waits=[f"qkev{c+1}"], fn=mk_featln(c + 1))

            em.op("tensor", f"phit1_{c}", waits=[f"phiq{c}", f"trsevq{c}"],
                  fn=mk_phit1(c))
            em.op("scalar", f"phitev1_{c}", waits=[f"phit1_{c}"], fn=mk_phitev1(c),
                  fence="once")
            if nxt:
                em.op("scalar", f"rsq{c+1}", waits=[f"featln{c+1}"], fn=mk_rsq(c + 1),
                      fence="once")
            em.op("tensor", f"amm2_{c}", waits=[f"asq1_{c}"], fn=mk_amm2(c))
            em.op("scalar", f"asq2_{c}", waits=[f"amm2_{c}"], fn=mk_asq2(c), fence="once")
            if nxt:
                em.op("vector", f"qc{c+1}", waits=[f"rsq{c+1}", f"gb{c}"], fn=mk_qc(c + 1))
                em.op("pool", f"gb{c+1}", waits=[f"qc{c+1}"], fn=mk_gb(c + 1))
                w = [f"qkvA{c+1}"] + ([f"smm_e{c-1}"] if c > 0 else [])
                em.op("scalar", f"vev1_{c+1}", waits=w, fn=mk_vev(c + 1, 0), fence="once")
            em.op("tensor", f"phit2_{c}", waits=[f"phitev1_{c}"], fn=mk_phit2(c))
            em.op("scalar", f"phitev2_{c}", waits=[f"phit2_{c}"], fn=mk_phitev2(c),
                  fence="once")
            em.op("vector", f"amask{c}", waits=[f"asq2_{c}"], fn=mk_amask(c), fence="once")
            if nxt:
                em.op("tensor", f"vh2_{c+1}", waits=[f"vev1_{c+1}"], fn=mk_vh2(c + 1))
                em.op("scalar", f"vev2_{c+1}", waits=[f"vh2_{c+1}"], fn=mk_vev(c + 1, 1),
                      fence="once")
            w = [f"phitev2_{c}", f"amask{c}", f"vev2_{c}"]
            w += [f"sbf{c-1}"] if c > 0 else ["init2p"]
            em.op("tensor", f"num{c}", waits=w, fn=mk_num(c))
            em.op("vector", f"y{c}", waits=[f"num{c}"], fn=mk_y(c))

            w = [f"phik{c}"]
            if c == 0:
                w += ["init2p"]
            else:
                w += [f"sadd_e{c-1}"]
            if nxt:
                w += [f"qkev{c+1}"]
            em.op("tensor", f"smm_a{c}", waits=w, fn=mk_smm(c, 0))
            em.op("vector", f"sadd_a{c}", waits=[f"smm_a{c}"] + (["init2p"] if c == 0 else []),
                  fn=mk_sadd(c, 0), fence="once")
            if nxt:
                em.op("vector", f"phiq{c+1}", waits=[f"gb{c+1}", f"phitev2_{c}"],
                      fn=mk_phiq(c + 1), fence="once")
                w = [f"gb{c+1}"] + ([f"smm_e{c-1}"] if c > 0 else [])
                em.op("pool", f"phik{c+1}", waits=w, fn=mk_phik(c + 1), fence="once")
            em.op("tensor", f"yt{c}", waits=[f"y{c}", f"phitev1_{c}"], fn=mk_yt(c))
            em.op("scalar", f"ytev{c}", waits=[f"yt{c}"], fn=mk_ytev(c), fence="once")
            em.op("tensor", f"smm_b{c}", waits=[f"vev2_{c+1}" if nxt else "vev2_15"],
                  fn=mk_smm(c, 1))
            em.op("vector", f"sadd_b{c}", waits=[f"smm_b{c}"], fn=mk_sadd(c, 1), fence="once")
            em.op("tensor", f"wo{c}", waits=[f"ytev{c}"], fn=mk_wo(c))
            em.op("scalar", f"sevac{c}", waits=[f"wo{c}"], fn=mk_sevac(c), fence="once")
            em.op("tensor", f"smm_c{c}", waits=[f"sadd_a{c}"], fn=mk_smm(c, 2))
            em.op("vector", f"sadd_c{c}", waits=[f"smm_c{c}"], fn=mk_sadd(c, 2), fence="once")
            em.op("tensor", f"smm_d{c}", waits=[f"y{c}"], fn=mk_smm(c, 3))
            em.op("vector", f"sadd_d{c}", waits=[f"smm_d{c}"], fn=mk_sadd(c, 3), fence="once")

        cl = ncks - 1
        em.op("tensor", f"smm_e{cl}", waits=[f"sadd_c{cl}"], fn=mk_smm(cl, 4))
        em.op("vector", f"sadd_e{cl}", waits=[f"smm_e{cl}"], fn=mk_sadd(cl, 4), fence="once")
        em.op("scalar", f"sbf{cl}", waits=[f"sadd_e{cl}"], fn=mk_sbf(cl), fence="once")

    def p2(em):
        def mk_g(hb, half, pbs):
            def f(e):
                par = hb % 2
                last = None
                for dk in range(8):
                    lhs = tb["wg_bf"][:, 1024 * par + 128 * dk:1024 * par + 128 * dk + 128]
                    for n in range(2):
                        tcol = 1024 * half + 512 * n
                        last = e.matmul(tb[pbs[n]][:, 0:512], lhs,
                                        tb["xnT"][:, 2048 * dk + tcol:2048 * dk + tcol + 512],
                                        start=dk == 0, stop=dk == 7)
                return last
            return f

        def mk_u(hb, half):
            def f(e):
                par = hb % 2
                last = None
                for dk in range(8):
                    lhs = tb["wu_bf"][:, 1024 * par + 128 * dk:1024 * par + 128 * dk + 128]
                    for n in range(2):
                        tcol = 1024 * half + 512 * n
                        last = e.matmul(tb[("PB2", "PB3")[n]][:, 0:512], lhs,
                                        tb["xnT"][:, 2048 * dk + tcol:2048 * dk + tcol + 512],
                                        start=dk == 0, stop=dk == 7)
                return last
            return f

        def mk_silu(hb, half, pbs):
            def f(e):
                ga = tb["g_act0"] if half == 0 else tb["g_act1"]
                e.activation(ga[:, 0:512], tb[pbs[0]][:], AF.Silu, scale=1.0 / 127.0)
                return e.activation(ga[:, 512:1024], tb[pbs[1]][:], AF.Silu,
                                    scale=1.0 / 127.0)
            return f

        def mk_hmul(hb, half):
            def f(e):
                ga = tb["g_act0"] if half == 0 else tb["g_act1"]
                last = None
                for n in range(2):
                    tcol = 2048 * hb + 1024 * half + 512 * n
                    last = e.tensor_tensor(tb["hT"][:, tcol:tcol + 512],
                                           ga[:, 512 * n:512 * n + 512],
                                           tb[("PB2", "PB3")[n]][:], ALU.mult)
                return last
            return f

        for hb in range(NHB):
            if hb >= 2:
                def f_gu(e, hb=hb):
                    par = hb % 2
                    e.dma_start(tb["wg_bf"][:, 1024 * par:1024 * par + 1024],
                                tb["d_wgt"][hb]).then_inc(em.sems[f"gu{hb % 2}"], 16)
                    e.dma_start(tb["wu_bf"][:, 1024 * par:1024 * par + 1024],
                                tb["d_wut"][hb]).then_inc(em.sems[f"gu{hb % 2}"], 16)
                em.op("pool", f"gu_dma{hb}", waits=[f"hmul1_{hb-2}"], fn=f_gu,
                      incs=32, self_inc=True, sem=f"gu{hb % 2}")

            w = [f"gu_dma{hb}"]
            if hb == 0:
                w += ["sevac15", "qkev15"]
            else:
                w += [f"silu0_{hb-1}"]
            em.op("tensor", f"tGU_g0_{hb}", waits=w, fn=mk_g(hb, 0, ("PB0", "PB1")))
            w = ["sadd_e15", "vev2_15"] if hb == 0 else [f"hmul1_{hb-1}"]
            em.op("tensor", f"tGU_u0_{hb}", waits=w, fn=mk_u(hb, 0))
            em.op("scalar", f"silu0_{hb}", waits=[f"tGU_g0_{hb}"],
                  fn=mk_silu(hb, 0, ("PB0", "PB1")), fence="once")
            em.op("vector", f"hmul0_{hb}", waits=[f"silu0_{hb}", f"tGU_u0_{hb}"],
                  fn=mk_hmul(hb, 0), fence="once")
            w = ["sadd_e15", "qkev15"] if hb == 0 else [f"silu1_{hb-1}"]
            em.op("tensor", f"tGU_g1_{hb}", waits=w, fn=mk_g(hb, 1, ("PB4", "PB5")))
            em.op("tensor", f"tGU_u1_{hb}", waits=[f"hmul0_{hb}"], fn=mk_u(hb, 1))
            em.op("scalar", f"silu1_{hb}", waits=[f"tGU_g1_{hb}"],
                  fn=mk_silu(hb, 1, ("PB4", "PB5")), fence="once")
            em.op("vector", f"hmul1_{hb}", waits=[f"silu1_{hb}", f"tGU_u1_{hb}"],
                  fn=mk_hmul(hb, 1), fence="once")
            if hb >= 18:
                for hk in (2 * (hb - 18), 2 * (hb - 18) + 1):
                    def f_wd(e, hk=hk):
                        e.dma_start(tb["wdTA"][:, 1024 * hk:1024 * hk + 1024],
                                    tb["d_wdt"][hk]).then_inc(em.sems["wdA"], 16)
                    em.op("pool", f"wd_dma{hk}", fn=f_wd, incs=16, self_inc=True,
                          sem="wdA" if hk < 12 else "wdB")

    def p3(em):
        for hk in range(12, NHB):
            def f_wd(e, hk=hk):
                e.dma_start(tb["wdTB"][:, 1024 * (hk - 12):1024 * (hk - 12) + 1024],
                            tb["d_wdt"][hk]).then_inc(em.sems["wdB"], 16)
            em.op("pool", f"wd_dma{hk}", fn=f_wd, incs=16, self_inc=True, sem="wdB")

        for c in range(NCK):
            def f_x2(e, c=c):
                e.dma_start(tb["x2"][:, 1024 * (c % 2):1024 * (c % 2) + 1024],
                            tb["d_x"][128 * c:128 * c + 128, :]).then_inc(em.sems[f"ox{c % 2}"], 16)
            w = [f"oadd{c-2}"] if c >= 2 else ["tGU_u1_23"]
            em.op("sync", f"x2dma{c}", waits=w, fn=f_x2, incs=16, self_inc=True, sem=f"ox{c % 2}")

            for sub in range(2):
                def f_dn(e, c=c, sub=sub):
                    pb = ("PB0", "PB1") if c % 2 == 0 else ("PB2", "PB3")
                    last = None
                    for hk in range(12 * sub, 12 * sub + 12):
                        lhs = tb["hT"][:, 2048 * hk + 128 * c:2048 * hk + 128 * c + 128]
                        wdt = tb["wdTA"] if hk < 12 else tb["wdTB"]
                        col = 1024 * (hk % 12)
                        for n in range(2):
                            last = e.matmul(tb[pb[n]][:, 0:512], lhs,
                                            wdt[:, col + 512 * n:col + 512 * n + 512],
                                            start=hk == 0, stop=hk == 23)
                    return last
                w = [f"wd_dma{12 * sub + 11}"]
                if sub == 0:
                    if c == 0:
                        w += ["hmul1_23", "silu0_23"]
                    if c >= 2:
                        w += [f"g{c-2}"]
                em.op("tensor", f"dn{sub}_{c}", waits=w, fn=f_dn)

            def f_g(e, c=c):
                pb = ("PB0", "PB1") if c % 2 == 0 else ("PB2", "PB3")
                ga = tb["g_st0"] if c % 2 == 0 else tb["g_st1"]
                e.activation(ga[:, 0:512], tb[pb[0]][:], AF.Silu, scale=1.0 / 16129.0)
                return e.activation(ga[:, 512:1024], tb[pb[1]][:], AF.Silu,
                                    scale=1.0 / 16129.0)
            w = [f"dn1_{c}"] + ([f"odma{c-2}"] if c >= 2 else [])
            em.op("scalar", f"g{c}", waits=w, fn=f_g, fence="once")

            def f_sg(e, c=c):
                ga = tb["g_st0"] if c % 2 == 0 else tb["g_st1"]
                return e.tensor_tensor(ga[:], ga[:], tb["s_sb"][:, 1024 * c:1024 * c + 1024],
                                       ALU.mult)
            em.op("vector", f"sg{c}", waits=[f"g{c}"], fn=f_sg, fence="once")

            def f_oadd(e, c=c):
                ga = tb["g_st0"] if c % 2 == 0 else tb["g_st1"]
                return e.tensor_tensor(ga[:], ga[:],
                                       tb["x2"][:, 1024 * (c % 2):1024 * (c % 2) + 1024],
                                       ALU.add)
            em.op("pool", f"oadd{c}", waits=[f"sg{c}", f"x2dma{c}"], fn=f_oadd, fence="once")

            def f_od(e, c=c):
                ga = tb["g_st0"] if c % 2 == 0 else tb["g_st1"]
                e.dma_start(tb["d_out"][128 * c:128 * c + 128, :],
                            ga[:]).then_inc(em.sems[f"ou{c % 2}"], 16)
            em.op("sync", f"odma{c}", waits=[f"oadd{c}"], fn=f_od, incs=16, self_inc=True, sem=f"ou{c % 2}")

        em.wait_only("sync", "fin", [f"odma{c}" for c in range(NCK)])

    return p0, attn, p2, p3


def _dbg_taps(em, tb, names):
    for nm, waits in names:
        def f(e, nm=nm):
            e.dma_start(tb["d_" + nm], tb[nm][:]).then_inc(em.sems["tap"], 16)
        em.op("sync", f"tap_{nm}", waits=waits, fn=f, incs=16, self_inc=True, sem="tap")
    if names:
        em.wait_only("sync", "tapfin", [f"tap_{nm}" for nm, _ in names])


_CACHED = {}


def _build(debug=False, phases=4, ncks=NCK, astages=9999):
    if (debug, phases, ncks, astages) in _CACHED:
        return _CACHED[(debug, phases, ncks, astages)]
    nc = bass.Bass("TRN2", target_bir_lowering=False)
    tb = {}
    tb["d_x"] = nc.dram_tensor("x", [T, D], F32, kind="ExternalInput").ap()
    tb["d_wqt"] = nc.dram_tensor("wqt", [128, 2048], F32, kind="ExternalInput").ap()
    tb["d_wkt"] = nc.dram_tensor("wkt", [128, 2048], F32, kind="ExternalInput").ap()
    tb["d_wvt"] = nc.dram_tensor("wvt", [128, 8192], F32, kind="ExternalInput").ap()
    tb["d_wot"] = nc.dram_tensor("wot", [128, 8192], F32, kind="ExternalInput").ap()
    tb["d_wgt"] = nc.dram_tensor("wgt", [NHB, 128, 1024], I32, kind="ExternalInput").ap()
    tb["d_wut"] = nc.dram_tensor("wut", [NHB, 128, 1024], I32, kind="ExternalInput").ap()
    tb["d_wdt"] = nc.dram_tensor("wdt", [NHB, 128, 1024], I32, kind="ExternalInput").ap()
    tb["d_gpad"] = nc.dram_tensor("gpad", [128, 512], F32, kind="ExternalInput").ap()
    tb["d_bpad"] = nc.dram_tensor("bpad", [128, 512], F32, kind="ExternalInput").ap()
    tb["d_identb"] = nc.dram_tensor("identb", [128, 128], BF16, kind="ExternalInput").ap()
    tb["d_maskb"] = nc.dram_tensor("maskb", [128, 128], BF16, kind="ExternalInput").ap()
    tb["d_out"] = nc.dram_tensor("out", [T, D], F32, kind="ExternalOutput").ap()
    dbg_attn = []
    dbg_p3 = []
    if debug:
        for nm, shp, dt in (("xnT", [128, 16384], BF16), ("s_sb", [128, 16384], BF16),
                            ("S_f32", [128, 2080], F32), ("S_bf", [128, 2080], BF16),
                            ("phik", [128, 8192], BF16), ("phiq", [128, 4096], BF16),
                            ("phiqT", [128, 4096], BF16), ("y_nat", [128, 1024], BF16),
                            ("qhat", [128, 2048], BF16), ("khd", [128, 256], BF16),
                            ("v_bf", [128, 2080], BF16), ("Asq", [128, 2048], BF16),
                            ("qhatT", [128, 2048], BF16), ("yT", [128, 1024], BF16)):
            tb["d_" + nm] = nc.dram_tensor("tap_" + nm, shp, dt, kind="ExternalOutput").ap()
        tb["d_hT"] = nc.dram_tensor("tap_hT", [128, 49152], BF16, kind="ExternalOutput").ap()
        last_attn = ["sevac15", "sbf15", "y15", "phitev2_15", "amask15",
                     "qkev15", "gb15", "vev2_15", "phiq15", "phik15"]
        dbg_attn = [(nm, last_attn) for nm in ("xnT", "s_sb", "S_f32", "S_bf", "phik",
                                               "phiq", "phiqT", "y_nat", "qhat", "khd",
                                               "v_bf", "Asq", "qhatT", "yT")]
        dbg_p3 = [("hT", ["hmul1_23"])]

    em = Em()
    p0f, attnf, p2f, p3f = _program(em, tb, debug, ncks=ncks, astages=astages)
    em.mode = "count"
    p0f(em)
    if phases >= 2:
        attnf(em)
        if debug:
            _dbg_taps(em, tb, dbg_attn)
    if phases >= 3:
        p2f(em)
        if debug:
            _dbg_taps(em, tb, dbg_p3)
    if phases >= 4:
        p3f(em)

    ctxs = []

    def enter(ctx):
        ctxs.append(ctx)
        return ctx.__enter__()

    sems = {}
    for e in SEM_NAMES:
        sems[e] = enter(nc.semaphore(f"sem_{e}"))
    em.sems = sems
    for i in range(6):
        tb[f"PB{i}"] = enter(nc.psum_tensor(f"pb{i}", [128, 512], F32))
    for i in (6, 7):
        tb[f"PB{i}"] = enter(nc.psum_tensor(f"pb{i}", [128, 1024], BF16))

    tb["s_sb"] = enter(nc.sbuf_tensor("s_sb", [128, 16384], BF16))
    tb["identb"] = enter(nc.sbuf_tensor("identb_sb", [128, 128], BF16))
    tb["maskb"] = enter(nc.sbuf_tensor("maskb_sb", [128, 128], BF16))
    tb["gpad"] = enter(nc.sbuf_tensor("gpad_sb", [128, 512], F32))
    tb["bpad"] = enter(nc.sbuf_tensor("bpad_sb", [128, 512], F32))
    tb["eps_sb"] = enter(nc.sbuf_tensor("eps_sb", [128, 1], F32))
    ctx_wg = nc.sbuf_tensor("wg_bf", [128, 2048], BF16)
    tb["wg_bf"] = ctx_wg.__enter__()
    ctx_wu = nc.sbuf_tensor("wu_bf", [128, 2048], BF16)
    tb["wu_bf"] = ctx_wu.__enter__()
    ctx_xnt = nc.sbuf_tensor("xnT", [128, 16384], BF16)
    tb["xnT"] = ctx_xnt.__enter__()

    def run_block(progfn, extra_dbg=None):
        with nc.Block() as blk:
            for ename in ("sync", "tensor", "vector", "scalar", "pool"):
                def body(eng_obj, ename=ename):
                    em.mode = "emit"
                    em.emit_eng = ename
                    em.obj = eng_obj
                    progfn(em)
                    if extra_dbg:
                        _dbg_taps(em, tb, extra_dbg)
                getattr(blk, "gpsimd" if ename == "pool" else ename)(body)

    aw_ctxs = []

    def enterAW(ctx):
        aw_ctxs.append(ctx)
        return ctx.__enter__()

    tb["wq_bf"] = enterAW(nc.sbuf_tensor("wq_bf", [128, 2048], BF16))
    tb["wk_bf"] = enterAW(nc.sbuf_tensor("wk_bf", [128, 2048], BF16))
    tb["wv_bf"] = enterAW(nc.sbuf_tensor("wv_bf", [128, 8192], BF16))
    tb["wo_bf"] = enterAW(nc.sbuf_tensor("wo_bf", [128, 8192], BF16))
    tb["phiqT"] = enterAW(nc.sbuf_tensor("phiqT", [128, 4096], BF16))
    tb["v_bf"] = enterAW(nc.sbuf_tensor("v_bf", [128, 2080], BF16))
    tb["S_f32"] = enterAW(nc.sbuf_tensor("S_f32", [128, 2080], F32))
    tb["S_bf"] = enterAW(nc.sbuf_tensor("S_bf", [128, 2080], BF16))

    p0_ctxs = []

    def enter0(ctx):
        p0_ctxs.append(ctx)
        return ctx.__enter__()

    tb["x_t"] = enter0(nc.sbuf_tensor("x_t", [128, 4096], F32))
    tb["xsq"] = enter0(nc.sbuf_tensor("xsq", [128, 2048], F32))
    tb["xn_nat"] = enter0(nc.sbuf_tensor("xn_nat", [128, 2048], BF16))
    tb["p0st"] = enter0(nc.sbuf_tensor("p0st", [128, 16], F32))
    run_block(p0f)
    for c in reversed(p0_ctxs):
        c.__exit__(None, None, None)

    a_ctxs = []

    def enterA(ctx):
        a_ctxs.append(ctx)
        return ctx.__enter__()

    tb["qsq"] = enterA(nc.sbuf_tensor("qsq", [128, 512], F32))
    tb["fst"] = enterA(nc.sbuf_tensor("fst", [128, 256], F32))
    tb["qc"] = enterA(nc.sbuf_tensor("qc", [128, 512], F32))
    tb["qhat"] = enterA(nc.sbuf_tensor("qhat", [128, 2048], BF16))
    tb["khat2"] = enterA(nc.sbuf_tensor("khat2", [128, 512], BF16))
    tb["khd"] = enterA(nc.sbuf_tensor("khd", [128, 256], BF16))
    tb["qhatT"] = enterA(nc.sbuf_tensor("qhatT", [128, 2048], BF16))
    tb["khatT2"] = enterA(nc.sbuf_tensor("khatT2", [128, 512], BF16))
    tb["phiq"] = enterA(nc.sbuf_tensor("phiq", [128, 4096], BF16))
    tb["phik"] = enterA(nc.sbuf_tensor("phik", [128, 8192], BF16))
    tb["Asq"] = enterA(nc.sbuf_tensor("Asq", [128, 2048], BF16))
    tb["qk_sb"] = enterA(nc.sbuf_tensor("qk_sb", [128, 512], F32))
    tb["y_nat"] = enterA(nc.sbuf_tensor("y_nat", [128, 1024], BF16))
    tb["yT"] = enterA(nc.sbuf_tensor("yT", [128, 1024], BF16))
    tb["den"] = enterA(nc.sbuf_tensor("den", [128, 32], F32))
    if phases >= 2:
        run_block(attnf, extra_dbg=dbg_attn if debug else None)
    for c in reversed(a_ctxs):
        c.__exit__(None, None, None)
    for c in reversed(aw_ctxs):
        c.__exit__(None, None, None)

    ctx_ht = nc.sbuf_tensor("hT", [128, 49152], BF16, side="right")
    tb["hT"] = ctx_ht.__enter__()
    ctx_wdta = nc.sbuf_tensor("wdTA", [128, 12288], BF16, side="right")
    tb["wdTA"] = ctx_wdta.__enter__()
    ctx_ga0 = nc.sbuf_tensor("g_act0", [128, 1024], F32)
    tb["g_act0"] = ctx_ga0.__enter__()
    ctx_ga1 = nc.sbuf_tensor("g_act1", [128, 1024], F32)
    tb["g_act1"] = ctx_ga1.__enter__()
    if phases >= 3:
        run_block(p2f, extra_dbg=dbg_p3 if debug else None)
    ctx_ga1.__exit__(None, None, None)
    ctx_ga0.__exit__(None, None, None)
    ctx_xnt.__exit__(None, None, None)
    ctx_wu.__exit__(None, None, None)
    ctx_wg.__exit__(None, None, None)

    c_ctxs = []

    def enterC(ctx):
        c_ctxs.append(ctx)
        return ctx.__enter__()

    tb["wdTB"] = enterC(nc.sbuf_tensor("wdTB", [128, 12288], BF16, side="right"))
    tb["g_st0"] = enterC(nc.sbuf_tensor("g_st0", [128, 1024], F32))
    tb["g_st1"] = enterC(nc.sbuf_tensor("g_st1", [128, 1024], F32))
    tb["x2"] = enterC(nc.sbuf_tensor("x2", [128, 2048], F32))
    if phases >= 4:
        run_block(p3f)
    for c in reversed(c_ctxs):
        c.__exit__(None, None, None)
    ctx_wdta.__exit__(None, None, None)
    ctx_ht.__exit__(None, None, None)
    for c in reversed(ctxs):
        c.__exit__(None, None, None)

    _CACHED[(debug,)] = nc
    return nc


def _host_inputs(Wq, Wk, Wv, Wo, gamma, beta, wg, wu, wd):
    f32 = np.float32
    def att_w(w, fcols):
        return np.ascontiguousarray(
            np.asarray(w, f32).T.reshape(8, 128, fcols).transpose(1, 0, 2).reshape(128, 8 * fcols))
    wqt = att_w(Wq, 256)
    wkt = att_w(Wk, 256)
    wvt = att_w(Wv, 1024)
    wot = att_w(Wo, 1024)
    wgt = np.ascontiguousarray(
        np.asarray(wg, np.int32).reshape(24, 128, 8, 128).transpose(0, 3, 2, 1).reshape(24, 128, 1024))
    wut = np.ascontiguousarray(
        np.asarray(wu, np.int32).reshape(24, 128, 8, 128).transpose(0, 3, 2, 1).reshape(24, 128, 1024))
    wdt = np.ascontiguousarray(np.asarray(wd, np.int32).T.reshape(24, 128, 1024))
    gpad = np.ascontiguousarray(
        np.broadcast_to(np.tile(np.asarray(gamma, f32), 32), (128, 512)))
    bpad = np.ascontiguousarray(
        np.broadcast_to(np.tile(np.asarray(beta, f32), 32), (128, 512)))
    identb = np.eye(128, dtype=BF)
    maskb = np.triu(np.ones((128, 128), f32)).astype(BF)
    return dict(wqt=wqt, wkt=wkt, wvt=wvt, wot=wot, wgt=wgt, wut=wut, wdt=wdt,
                gpad=gpad, bpad=bpad, identb=identb, maskb=maskb)


def kernel(L0_attn_hidden_states, L0_attn_Wq, L0_attn_Wk, L0_attn_Wv, L0_attn_Wo,
           L0_attn_gamma, L0_attn_beta, L0_ffn_w_gate, L0_ffn_w_up, L0_ffn_w_down):
    x = np.ascontiguousarray(np.asarray(L0_attn_hidden_states, np.float32))
    shared = _host_inputs(L0_attn_Wq, L0_attn_Wk, L0_attn_Wv, L0_attn_Wo,
                          L0_attn_gamma, L0_attn_beta,
                          L0_ffn_w_gate, L0_ffn_w_up, L0_ffn_w_down)
    nc = _build(debug=False)
    in_maps = [dict(shared, x=np.ascontiguousarray(x[i])) for i in range(8)]
    res = run_bass_kernel_spmd(nc, in_maps, core_ids=list(range(8)))
    return np.stack([res.results[i]["out"] for i in range(8)], axis=0)
